# version 22
# baseline (speedup 1.0000x reference)
import math
from contextlib import ExitStack
import numpy as np
import ml_dtypes
import concourse.bass as bass
import concourse.mybir as mybir
from concourse.bass import ds
from concourse.bass_utils import run_bass_kernel_spmd
from concourse.replica_groups import maybe_share_collective_output_space

F32 = mybir.dt.float32
BF16 = mybir.dt.bfloat16
AF = mybir.ActivationFunctionType
ALU = mybir.AluOpType
bfnp = ml_dtypes.bfloat16

NC = 8
NB = 2
SEQ = 4096
E = 2048
H = 16
D = 128
ROWS = 1024
KCH = 16
RG = [list(range(NC))]
EPS = 1e-5
SQ8 = 1.0 / math.sqrt(8.0)

DEBUG = set()
STAGES = 9
PARANOID = ""


class Sem:
    def __init__(self, h, name=""):
        self.h = h
        self.n = 0
        self.name = name


class Step:
    __slots__ = ("eng", "fn", "waits", "incs")
    def __init__(self, eng, fn, waits, incs):
        self.eng, self.fn, self.waits, self.incs = eng, fn, waits, incs


class Sched:
    def __init__(self, marks=(), allsems=None):
        self.steps = []
        self.marks = list(marks)
        self._seen = set()
        self.allsems = allsems

    SELF = {"vector": "VC", "scalar": "SC", "gpsimd": "PC"}

    def mark_section(self, marks):
        self._sec_marks = list(marks)
        self._sec_seen = set()

    def add(self, eng, fn, waits=(), inc=None, inck=1):
        waits = list(waits)
        if self.allsems is not None:
            nm = Sched.SELF.get(eng)
            if nm is not None:
                waits += [(x, x.n) for x in self.allsems if x.name == nm]
            if PARANOID and (PARANOID == "all" or eng in PARANOID):
                waits += [(x, x.n) for x in self.allsems]
        w = [(s, v) for (s, v) in waits if v]
        if eng not in self._seen:
            self._seen.add(eng)
            w = self.marks + w
        sm = getattr(self, "_sec_marks", None)
        if sm is not None and eng not in self._sec_seen:
            self._sec_seen.add(eng)
            w = [(x, v) for (x, v) in sm if v] + w
        incs = []
        if inc is not None:
            incs.append((inc, inck))
            inc.n += inck
        self.steps.append(Step(eng, fn, w, incs))
        return inc.n if inc is not None else None

    def emit(self, nc):
        with nc.Block() as block:
            for name in ("sync", "scalar", "vector", "tensor", "gpsimd"):
                mine = [s for s in self.steps if s.eng == name]
                if not mine:
                    continue
                def section(eng, mine=mine):
                    for st in mine:
                        seen = {}
                        for (sem, val) in st.waits:
                            if seen.get(id(sem), -1) < val:
                                seen[id(sem)] = val
                                eng.wait_ge(sem.h, val)
                        inst = st.fn(eng)
                        for (sem, k) in st.incs:
                            inst.then_inc(sem.h, k)
                getattr(block, name)(section)


def marks_of(sems):
    return [(s, s.n) for s in sems if s.n > 0]


def build(stack):
    nc = bass.Bass("TRN2", target_bir_lowering=False, debug=False)

    x_rows = nc.dram_tensor("x_rows", [ROWS, E], F32, kind="ExternalInput").ap()
    wqkv = nc.dram_tensor("wqkv", [E, 768], F32, kind="ExternalInput").ap()
    wout = nc.dram_tensor("wout", [E, E], F32, kind="ExternalInput").ap()
    facs = nc.dram_tensor("facs", [8, E, 256], F32, kind="ExternalInput").ap()
    fbias = nc.dram_tensor("fbias", [1, E], F32, kind="ExternalInput").ap()
    lscale = nc.dram_tensor("lscale", [D, 1], F32, kind="ExternalInput").ap()
    lbias = nc.dram_tensor("lbias", [D, 1], F32, kind="ExternalInput").ap()
    rcos = nc.dram_tensor("rcos", [SEQ, 64], BF16, kind="ExternalInput").ap()
    rsin = nc.dram_tensor("rsin", [SEQ, 64], BF16, kind="ExternalInput").ap()
    ident_f = nc.dram_tensor("ident_f", [128, 128], F32, kind="ExternalInput").ap()
    ident_b = nc.dram_tensor("ident_b", [128, 128], BF16, kind="ExternalInput").ap()
    onescol = nc.dram_tensor("onescol", [128, 1], BF16, kind="ExternalInput").ap()
    onesrow = nc.dram_tensor("onesrow", [1, 128], F32, kind="ExternalInput").ap()
    constv = nc.dram_tensor("constv", [128, 4], F32, kind="ExternalInput").ap()
    out_rows = nc.dram_tensor("out_rows", [ROWS, E], F32, kind="ExternalOutput").ap()

    sh_ag = maybe_share_collective_output_space("AllGather", RG)
    cc1_in = [nc.dram_tensor(f"cc1_in{h}", [E, ROWS // 2], BF16, kind="Internal").ap() for h in range(2)]
    cc1_out = [nc.dram_tensor(f"cc1_out{h}", [NC * E, ROWS // 2], BF16, kind="Internal", addr_space=sh_ag).ap() for h in range(2)]
    cc2_in = nc.dram_tensor("cc2_in", [E, 256], BF16, kind="Internal").ap()
    cc2_out = nc.dram_tensor("cc2_out", [NC * E, 256], BF16, kind="Internal", addr_space=sh_ag).ap()
    cc3_in = nc.dram_tensor("cc3_in", [E, ROWS], BF16, kind="Internal").ap()
    cc3_out = nc.dram_tensor("cc3_out", [NC * E, ROWS], BF16, kind="Internal", addr_space=sh_ag).ap()

    dbg = {}
    for name, shape, dt in (
        ("d_xnT", [128, KCH, ROWS], BF16),
        ("d_qkT0", [128, 4, SEQ], BF16),
        ("d_qkT1", [128, 4, SEQ], BF16),
        ("d_v0", [128, 2, 32, 128], BF16),
        ("d_v1", [128, 2, 32, 128], BF16),
        ("d_ob0", [128, 8, 512], BF16),
        ("d_ob1", [128, 8, 512], BF16),
        ("d_ob2", [128, 8, 512], BF16),
        ("d_ob3", [128, 8, 512], BF16),
        ("d_attnT", [128, H, ROWS], BF16),
        ("d_wsum", [128, KCH, 1024], BF16),
        ("d_wout", [128, KCH, 1024], BF16),
        ("d_cc2", [NC * E, 256], BF16),
        ("d_wtmp", [128, KCH, 256], BF16),
        ("d_faca", [128, KCH, 256], F32),
    ):
        if name in DEBUG:
            dbg[name] = nc.dram_tensor(name, shape, dt, kind="ExternalOutput").ap()

    def mksem(name):
        return Sem(stack.enter_context(nc.semaphore(name)), name)
    TC, VC, SC, PC = mksem("TC"), mksem("VC"), mksem("SC"), mksem("PC")
    SDX = mksem("SDX")
    ADC = mksem("ADC")
    WQ = mksem("WQ")
    FA = mksem("FA")
    FB0, FB1, FB2 = mksem("FB0"), mksem("FB1"), mksem("FB2")
    CC1W, CC2W, CC3W = mksem("CC1W"), mksem("CC2W"), mksem("CC3W")
    CCA1, CCA2, CCA3 = mksem("CCA1"), mksem("CCA2"), mksem("CCA3")
    XB0, XB1 = mksem("XB0"), mksem("XB1")
    WO = mksem("WO")
    WS = mksem("WS")
    AT = mksem("AT")
    OT0, OT1 = mksem("OT0"), mksem("OT1")
    X50, X51 = mksem("X50"), mksem("X51")
    DBG = mksem("DBG")
    ALLSEMS = (TC, VC, SC, PC, SDX, ADC, WQ, FA, FB0, FB1, FB2, CC1W, CC2W, CC3W,
               CCA1, CCA2, CCA3, XB0, XB1, WO, WS, AT, OT0, OT1, X50, X51, DBG)

    def _dev_final_barrier():
        sf = Sched({}, allsems=ALLSEMS)
        sf.add("sync", lambda e: e.nop(), waits=[(S, S.n) for S in ALLSEMS if S.n > 0])
        sf.emit(nc)


    sb = lambda n, s, d: stack.enter_context(nc.sbuf_tensor(n, s, d))

    id_f = sb("id_f", [128, 128], F32)
    id_b = sb("id_b", [128, 128], BF16)
    ones_c = sb("ones_c", [128, 1], BF16)
    ones_r = sb("ones_r", [1, 128], F32)
    ls_c = sb("ls_c", [128, 1], F32)
    lb_c = sb("lb_c", [128, 1], F32)
    xnT_own = sb("xnT_own", [128, KCH, ROWS], BF16)
    cst = sb("cst", [128, 4], F32)
    nc.const_aps.register(F32, EPS, cst[:, 0:1])
    nc.const_aps.register(F32, SQ8, cst[:, 1:2])

    stageA = ExitStack()
    sbA = lambda n, s, d: stageA.enter_context(nc.sbuf_tensor(n, s, d, side="right"))
    cos_sb = sbA("cos_sb", [128, 32, 64], BF16)
    sin_sb = sbA("sin_sb", [128, 32, 64], BF16)
    wqkv_sb = sbA("wqkv_sb", [128, KCH, 768], BF16)

    s1 = Sched(allsems=ALLSEMS)
    s1.add("scalar", lambda e: e.dma_start(out=cos_sb[:], in_=rcos.rearrange("(t p) f -> p t f", p=128)), inc=ADC, inck=16)
    s1.add("scalar", lambda e: e.dma_start(out=sin_sb[:], in_=rsin.rearrange("(t p) f -> p t f", p=128)), inc=ADC, inck=16)
    s1.add("scalar", lambda e: e.dma_start(out=id_f[:], in_=ident_f[:]), inc=ADC, inck=16)
    s1.add("scalar", lambda e: e.dma_start(out=id_b[:], in_=ident_b[:]), inc=ADC, inck=16)
    s1.add("scalar", lambda e: e.dma_start(out=ones_c[:], in_=onescol[:]), inc=ADC, inck=16)
    s1.add("scalar", lambda e: e.dma_start(out=ones_r[:], in_=onesrow[:]), inc=ADC, inck=16)
    s1.add("scalar", lambda e: e.dma_start(out=ls_c[:], in_=lscale[:]), inc=ADC, inck=16)
    s1.add("scalar", lambda e: e.dma_start(out=lb_c[:], in_=lbias[:]), inc=ADC, inck=16)
    s1.add("scalar", lambda e: e.dma_start(out=cst[:], in_=constv[:]), inc=ADC, inck=16)
    ad_consts = ADC.n

    sbR = lambda n, sh, d: nc.sbuf_tensor(n, sh, d, side="right")
    with nc.sbuf_tensor("x_sb", [128, 2, E], F32) as x_sb, \
         nc.sbuf_tensor("st1", [128, 24], F32) as st1, \
         nc.sbuf_tensor("mv1", [128, 2], F32) as mv1, \
         nc.sbuf_tensor("rstd", [128, 1], F32) as rstd, \
         nc.sbuf_tensor("nbias", [128, 1], F32) as nbias, \
         nc.sbuf_tensor("sdv", [128, 1], F32) as sdv, \
         nc.sbuf_tensor("xnbf", [128, 2, E], BF16) as xnbf, \
         sbR("faca", [128, KCH, 256], F32) as faca, \
         sbR("facb", [128, 3, KCH, 256], F32) as facb, \
         nc.psum_tensor("tp1", [128, 2, 1024], BF16) as tp1:

        s1.add("gpsimd", lambda e: e.dma_start(out=wqkv_sb[:], in_=wqkv.rearrange("(k p) c -> p k c", p=128)), inc=WQ, inck=16)
        wq_mark = (WQ, WQ.n)
        fbsem = {0: FB0, 1: FB1, 2: FB2}
        ad_fac = {}
        pc_add = {0: 0}

        def fac_dma(r):
            if r == 0:
                s1.add("gpsimd", lambda e: e.dma_start(out=faca[:], in_=facs[0].rearrange("(k p) c -> p k c", p=128)), inc=FA, inck=16)
                return
            slot = (r - 1) % 3
            w = [(PC, pc_add.get(r - 3, 0))]
            s1.add("gpsimd", lambda e, r=r, slot=slot: e.dma_start(
                out=facb[:, slot], in_=facs[r].rearrange("(k p) c -> p k c", p=128)), waits=w, inc=fbsem[slot], inck=16)
            ad_fac[r] = (fbsem[slot], fbsem[slot].n)

        x5sems = {0: X50, 1: X51}
        sd_xt = {}
        sc_applyd = {}

        def x_dma(t):
            xs = t % 2
            w = [(SC, sc_applyd.get(t - 2, 0))]
            s1.add("sync", lambda e, t=t, xs=xs: e.dma_start(
                out=x_sb[:, xs], in_=x_rows[128 * t:128 * (t + 1), :]), waits=w, inc=x5sems[xs], inck=16)
            sd_xt[t] = (x5sems[xs], x5sems[xs].n)

        x_dma(0)
        x_dma(1)
        vc_evac = {0: 0, 1: 0}
        tc_xn = {0: 0, 1: 0}
        sc_prev_apply = 0
        for t in range(8):
            xs = t % 2
            for s4 in range(4):
                w = [sd_xt[t], (VC, VC.n), (SC, SC.n)] if s4 == 0 else []
                s1.add("vector", lambda e, xs=xs, s4=s4: e.bn_stats(st1[:, 6 * s4:6 * s4 + 6], x_sb[:, xs, 512 * s4:512 * (s4 + 1)]),
                       waits=w, inc=VC if s4 == 3 else None)
            vc_st = VC.n
            s1.add("vector", lambda e: e.bn_aggr(mv1[:], st1[:]), waits=[(VC, vc_st)], inc=VC)
            vc_mv = VC.n
            s1.add("scalar", lambda e: e.activation(sdv[:], mv1[:, 1:2], AF.Sqrt, bias=EPS),
                   waits=[(VC, vc_mv), (ADC, ad_consts)], inc=SC)
            sc_sd = SC.n
            s1.add("vector", lambda e: e.reciprocal(rstd[:], sdv[:]), waits=[(SC, max(sc_sd, sc_prev_apply))], inc=VC)
            vc_r = VC.n
            s1.add("vector", lambda e: e.tensor_scalar(nbias[:], mv1[:, 0:1], rstd[:], -1.0, ALU.mult, ALU.mult),
                   waits=[(VC, vc_r)], inc=VC)
            vc_nb = VC.n
            s1.add("scalar", lambda e, xs=xs: e.activation(xnbf[:, xs], x_sb[:, xs], AF.Identity, bias=nbias[:], scale=rstd[:]),
                   waits=[(VC, vc_nb), (TC, tc_xn[xs])], inc=SC)
            sc_apply = SC.n
            sc_prev_apply = sc_apply
            sc_applyd[t] = sc_apply
            if t + 2 < 8:
                x_dma(t + 2)
            for g4 in range(4):
                bank = g4 % 2
                for j in range(4):
                    kc = g4 * 4 + j
                    w = [(SC, sc_apply), (VC, vc_evac[bank]), (ADC, ad_consts)] if j == 0 else []
                    s1.add("tensor", lambda e, xs=xs, kc=kc, bank=bank, j=j: e.transpose(
                        tp1[:, bank, 128 * j:128 * (j + 1)], xnbf[:, xs, 128 * kc:128 * (kc + 1)], id_b[:]),
                        waits=w, inc=TC if j == 3 else None)
                tc_tp = TC.n
                if g4 == 3:
                    tc_xn[xs] = tc_tp
                s1.add("vector", lambda e, t=t, g4=g4, bank=bank: e.tensor_copy(
                    xnT_own[:, 4 * g4:4 * g4 + 4, 128 * t:128 * (t + 1)],
                    tp1[:, bank, 0:512].rearrange("p (j n) -> p j n", n=128)),
                    waits=[(TC, tc_tp)], inc=VC)
                vc_evac[bank] = VC.n
            if t == 3:
                vc_half = VC.n
                s1.add("sync", lambda e: e.dma_start(
                    out=cc1_in[0].rearrange("(k p) n -> p k n", p=128),
                    in_=xnT_own[:, :, 0:512]), waits=[(VC, vc_half)], inc=CC1W, inck=16)
                cc1w_a = CC1W.n
        vc_xnT = VC.n
        sc_LN_end = SC.n
        tc_s1_end = TC.n
        s1.add("sync", lambda e: e.dma_start(
            out=cc1_in[1].rearrange("(k p) n -> p k n", p=128),
            in_=xnT_own[:, :, 512:1024]), waits=[(VC, vc_xnT)], inc=CC1W, inck=16)
        cc1w_b = CC1W.n
        for r in range(4):
            fac_dma(r)
        for r in range(1, 8):
            w = [ad_fac[r], (PC, pc_add[r - 1])]
            if r == 1:
                w.append((FA, 16))
            s1.add("gpsimd", lambda e, slot=(r - 1) % 3: e.tensor_tensor(faca[:], faca[:], facb[:, slot], ALU.add),
                   waits=w, inc=PC)
            pc_add[r] = PC.n
            if r + 3 <= 7:
                fac_dma(r + 3)
            if r == 4:
                s1.add("gpsimd", lambda e: e.collective_compute("AllGather", ALU.bypass, ins=[cc1_in[0]], outs=[cc1_out[0]], replica_groups=RG),
                       waits=[(CC1W, cc1w_a)], inc=CCA1)
                cc_ag1a = CCA1.n
        s1.add("gpsimd", lambda e: e.collective_compute("AllGather", ALU.bypass, ins=[cc1_in[1]], outs=[cc1_out[1]], replica_groups=RG),
               waits=[(CC1W, cc1w_b)], inc=CCA1)
        cc_ag1b = CCA1.n
        if "d_xnT" in dbg:
            s1.add("sync", lambda e: e.dma_start(out=dbg["d_xnT"], in_=xnT_own[:]), waits=[(VC, vc_xnT)], inc=DBG, inck=16)
        cc_ag1 = cc_ag1b
        s1.add("gpsimd", lambda e: e.dma_start(out=cc2_in.rearrange("(k p) c -> p k c", p=128), in_=faca[:]),
               waits=[(PC, pc_add[7])], inc=CC2W, inck=16)
        sd_cc2 = CC2W.n
        s1.add("gpsimd", lambda e: e.collective_compute("AllGather", ALU.bypass, ins=[cc2_in], outs=[cc2_out], replica_groups=RG),
               waits=[(CC2W, sd_cc2)], inc=CCA2)
        cc_ag2 = CCA2.n
        if STAGES <= 2:
            s1.emit(nc)
    if STAGES <= 2:
        _dev_final_barrier()
        stageA.close()
        return nc

    stageB = ExitStack()
    sbB = lambda n, s, d: stageB.enter_context(nc.sbuf_tensor(n, s, d))
    qkT = [sbB(f"qkT{b}", [128, 4, SEQ], BF16) for b in range(NB)]
    vbuf = [sbB(f"vbuf{b}", [128, 2, 32, 128], BF16) for b in range(NB)]

    s2_marks = [(VC, vc_xnT), (SC, sc_LN_end), (TC, tc_s1_end), wq_mark]
    s2 = s1
    s2.mark_section(s2_marks)
    with nc.sbuf_tensor("xbuf", [128, 2, KCH, 256], BF16) as xbuf, \
         nc.sbuf_tensor("qk_r", [128, 3, 4, 128], F32) as qk_r, \
         nc.sbuf_tensor("tm_c", [128, 4, 2, 64], F32) as tm_c, \
         nc.sbuf_tensor("tm_s", [128, 4, 2, 64], F32) as tm_s, \
         nc.sbuf_tensor("hst", [128, 24], F32) as hst, \
         nc.sbuf_tensor("hmv", [128, 4, 2], F32) as hmv, \
         nc.sbuf_tensor("hsd", [128, 4], F32) as hsd, \
         nc.sbuf_tensor("hr", [128, 4], F32) as hr, \
         nc.sbuf_tensor("hnb", [128, 4], F32) as hnb, \
         nc.sbuf_tensor("u_n", [128, 4, 4, 128], F32) as u_n, \
         nc.psum_tensor("p_qk", [128, 3, 512], F32) as p_qk, \
         nc.psum_tensor("p_v", [128, 2, 512], F32) as p_v, \
         nc.psum_tensor("p_tp", [128, 2, 512], F32) as p_tp:

        sd_blk = {}
        tc_blkdone = {}
        vc_ropepsum = {}
        sc_vdone = {}
        sc_tpevac = {0: 0, 1: 0}
        tc_tpdone = {}
        sc_undone = {0: 0, 1: 0, 2: 0}
        sc_un_latest = [0]
        sc_unr = {}

        NBLK = 32

        xbsem = {0: XB0, 1: XB1}

        def xbuf_dma(blk):
            r, q4 = blk // 4, blk % 4
            slot = blk % 2
            half = q4 // 2
            w = [(CCA1, cc_ag1a if half == 0 else cc_ag1b), (TC, tc_blkdone.get(blk - 2, 0))]
            s2.add("sync", lambda e, r=r, q4=q4, slot=slot, half=half: e.dma_start(
                out=xbuf[:, slot],
                in_=cc1_out[half][r * E:(r + 1) * E, 256 * (q4 % 2):256 * (q4 % 2 + 1)].rearrange("(k p) n -> p k n", p=128)),
                waits=w, inc=xbsem[slot], inck=16)
            sd_blk[blk] = (xbsem[slot], xbsem[slot].n)

        xbuf_dma(0)
        xbuf_dma(1)

        def emit_tp(rcg2):
            u4 = rcg2 % 4
            tpb = rcg2 % 2
            blk2 = rcg2 // 2
            r2, q42 = blk2 // 4, blk2 % 4
            b2 = r2 // 4
            nch2 = ((r2 % 4) * 1024 + q42 * 256 + (rcg2 % 2) * 128) // 128
            for g in range(4):
                w = [(SC, max(sc_unr[rcg2], sc_tpevac[tpb]))] if g == 0 else []
                s2.add("tensor", lambda e, g=g, u4=u4, tpb=tpb: e.transpose(
                    p_tp[:, tpb, 128 * g:128 * (g + 1)], u_n[:, u4, g], id_f[:]),
                    waits=w, inc=TC if g == 3 else None)
            tc_tp = TC.n
            tc_tpdone[rcg2] = tc_tp
            qkT_dst = qkT[b2].rearrange("p g (c n) -> p g c n", n=128)[:, :, nch2]
            s2.add("scalar", lambda e, qkT_dst=qkT_dst, tpb=tpb: e.activation(
                qkT_dst, p_tp[:, tpb].rearrange("p (g n) -> p g n", n=128),
                AF.Identity, bias=lb_c[:], scale=ls_c[:]), waits=[(TC, tc_tp)], inc=SC)
            sc_tpevac[tpb] = SC.n

        for blk in range(NBLK):
            r, q4 = blk // 4, blk % 4
            slot = blk % 2
            b = r // 4
            for rc in range(2):
                rcg = blk * 2 + rc
                n0 = (r % 4) * 1024 + q4 * 256 + rc * 128
                nch = n0 // 128
                pb = rcg % 3
                w = [sd_blk[blk], (SC, sc_vdone.get(rcg - 2, 0)), wq_mark,
                     (VC, vc_ropepsum.get(rcg - 3, 0))]
                for k in range(KCH):
                    s2.add("tensor", lambda e, slot=slot, k=k, rc=rc, pb=pb: e.matmul(
                        p_qk[:, pb], xbuf[:, slot, k, 128 * rc:128 * (rc + 1)], wqkv_sb[:, k, 0:512],
                        start=(k == 0), stop=(k == KCH - 1)), waits=w if k == 0 else [])
                pv2 = rcg % 2
                for k in range(KCH):
                    s2.add("tensor", lambda e, slot=slot, k=k, rc=rc, pv2=pv2: e.matmul(
                        p_v[:, pv2, 0:256], xbuf[:, slot, k, 128 * rc:128 * (rc + 1)], wqkv_sb[:, k, 512:768],
                        start=(k == 0), stop=(k == KCH - 1)), inc=TC if k == KCH - 1 else None)
                tc_qkv = TC.n
                if rc == 1:
                    tc_blkdone[blk] = tc_qkv
                pq4 = p_qk[:, pb].rearrange("p (g t f) -> p g t f", t=2, f=64)
                cb = cos_sb[:, nch].unsqueeze(1).unsqueeze(2).broadcast_to([128, 4, 2, 64])
                sbn = sin_sb[:, nch].unsqueeze(1).unsqueeze(2).broadcast_to([128, 4, 2, 64])
                s2.add("vector", lambda e, pq4=pq4, cb=cb: e.tensor_tensor(tm_c[:], pq4, cb, ALU.mult),
                       waits=[(TC, tc_qkv)])
                s2.add("vector", lambda e, pq4=pq4, sbn=sbn: e.tensor_tensor(tm_s[:], pq4, sbn, ALU.mult), inc=VC)
                vc_m = VC.n
                vc_ropepsum[rcg] = vc_m
                qr4 = qk_r[:, pb].rearrange("p g (t f) -> p g t f", t=2, f=64)
                s2.add("vector", lambda e, qr4=qr4: e.tensor_tensor(qr4[:, :, 0], tm_c[:, :, 0], tm_s[:, :, 1], ALU.subtract),
                       waits=[(VC, vc_m), (SC, sc_undone[pb])])
                s2.add("vector", lambda e, qr4=qr4: e.tensor_tensor(qr4[:, :, 1], tm_s[:, :, 0], tm_c[:, :, 1], ALU.add), inc=VC)
                vc_rope = VC.n
                for g in range(4):
                    w = [(VC, vc_rope), (VC, VC.n), (SC, SC.n)] if g == 0 else []
                    s2.add("vector", lambda e, g=g, pb=pb: e.bn_stats(hst[:, 6 * g:6 * g + 6], qk_r[:, pb, g]),
                           waits=w, inc=VC if g == 3 else None)
                vc_hst = VC.n
                for g in range(4):
                    s2.add("vector", lambda e, g=g: e.bn_aggr(hmv[:, g], hst[:, 6 * g:6 * g + 6]),
                           waits=[(VC, vc_hst)] if g == 0 else [], inc=VC if g == 3 else None)
                vc_hmv = VC.n
                s2.add("scalar", lambda e: e.activation(hsd[:], hmv[:, :, 1], AF.Sqrt, bias=EPS),
                       waits=[(VC, vc_hmv)], inc=SC)
                sc_hsd = SC.n
                s2.add("vector", lambda e: e.reciprocal(hr[:], hsd[:]),
                       waits=[(SC, sc_hsd), (SC, sc_un_latest[0])], inc=VC)
                vc_hr = VC.n
                s2.add("vector", lambda e: e.tensor_scalar(hnb[:], hmv[:, :, 0], -1.0, None, ALU.mult),
                       waits=[(VC, vc_hr)])
                s2.add("vector", lambda e: e.tensor_tensor(hnb[:], hnb[:], hr[:], ALU.mult), inc=VC)
                vc_hnb = VC.n
                u4 = rcg % 4
                for g in range(4):
                    w = [(VC, vc_hnb), (TC, tc_tpdone.get(rcg - 4, 0))] if g == 0 else []
                    s2.add("scalar", lambda e, g=g, pb=pb, u4=u4: e.activation(
                        u_n[:, u4, g], qk_r[:, pb, g], AF.Identity, bias=hnb[:, g:g + 1], scale=hr[:, g:g + 1]),
                        waits=w, inc=SC if g == 3 else None)
                sc_un = SC.n
                sc_undone[pb] = sc_un
                sc_un_latest[0] = sc_un
                sc_unr[rcg] = sc_un
                s2.add("scalar", lambda e, b=b, nch=nch, pv2=pv2: e.activation(
                    vbuf[b][:, :, nch], p_v[:, pv2, 0:256].rearrange("p (h n) -> p h n", n=128), AF.Copy),
                    waits=[(TC, tc_qkv)], inc=SC)
                sc_vdone[rcg] = SC.n
                if rcg >= 2:
                    emit_tp(rcg - 2)
            if blk + 2 < NBLK:
                xbuf_dma(blk + 2)
        emit_tp(NBLK * 2 - 2)
        emit_tp(NBLK * 2 - 1)

        if "d_qkT0" in dbg:
            scv = SC.n
            s2.add("sync", lambda e: e.dma_start(out=dbg["d_qkT0"], in_=qkT[0][:]), waits=[(SC, scv)], inc=DBG, inck=16)
            s2.add("sync", lambda e: e.dma_start(out=dbg["d_qkT1"], in_=qkT[1][:]), inc=DBG, inck=16)
        if "d_v0" in dbg:
            scv = SC.n
            s2.add("sync", lambda e: e.dma_start(out=dbg["d_v0"], in_=vbuf[0][:]), waits=[(SC, scv)], inc=DBG, inck=16)
            s2.add("sync", lambda e: e.dma_start(out=dbg["d_v1"], in_=vbuf[1][:]), inc=DBG, inck=16)
        s2.emit(nc)
    stageA.close()
    if STAGES <= 3:
        _dev_final_barrier()
        stageB.close()
        return nc

    stageC = ExitStack()
    obuf = [stageC.enter_context(nc.sbuf_tensor(f"obuf{i}", [128, 8, 512], BF16, side="right")) for i in range(4)]

    s3 = Sched(marks_of(ALLSEMS), allsems=ALLSEMS)
    with nc.sbuf_tensor("s2a", [128, 4, 2, 512], F32) as s2a, \
         nc.sbuf_tensor("s2m", [128, 2, 512], F32) as s2m, \
         nc.sbuf_tensor("atr", [128, 4, 2, 512], BF16) as atr, \
         nc.sbuf_tensor("dsb", [1, 2, 512], F32) as dsb, \
         nc.sbuf_tensor("dr", [1, 2, 512], F32) as dr, \
         nc.sbuf_tensor("rb", [128, 2, 512], F32) as rb, \
         nc.psum_tensor("p_s", [128, 4, 512], F32) as p_s, \
         nc.psum_tensor("p_o", [128, 2, 512], F32) as p_o, \
         nc.psum_tensor("p_d", [1, 512], F32) as p_d, \
         nc.psum_tensor("p_b", [128, 512], F32) as p_b:

        sc_sq = {}
        pc_mask = {}
        vc_p2 = {}
        tc_den = {}
        tc_st = {}
        s2m_user = 0
        vc_ob = {0: 0, 1: 0}
        vc_dr_prev = 0
        tailq = {}
        gp = 0

        tc_den_last = {}
        vc_recip = {}
        tc_bcst = {}
        sc_rb = {}
        vc_stt = {}

        def emit_evac(gq):
            s3.add("scalar", lambda e, gq=gq: e.activation(dsb[:, gq % 2], p_d[:], AF.Copy),
                   waits=[(TC, tc_den_last[gq]), (VC, vc_recip.get(gq - 2, 0))], inc=SC)
            tailq[gq] = SC.n

        def emit_recip(gq):
            s3.add("vector", lambda e, gq=gq: e.reciprocal(dr[:, gq % 2], dsb[:, gq % 2]),
                   waits=[(SC, tailq[gq]), (TC, tc_bcst.get(gq - 2, 0))], inc=VC)
            vc_recip[gq] = VC.n

        def emit_tail(bh, qb):
            po = qb % 2
            gq = bh * 8 + qb
            s3.add("tensor", lambda e, gq=gq: e.matmul(p_b[:], ones_r[:], dr[:, gq % 2], start=True, stop=True),
                   waits=[(VC, vc_recip[gq]), (SC, sc_rb.get(gq - 1, 0))], inc=TC)
            tc_bcst[gq] = TC.n
            s3.add("scalar", lambda e, gq=gq: e.activation(rb[:, gq % 2], p_b[:], AF.Copy),
                   waits=[(TC, tc_bcst[gq]), (VC, vc_stt.get(gq - 2, 0))], inc=SC)
            sc_rb[gq] = SC.n
            s3.add("vector", lambda e, bh=bh, qb=qb, po=po, gq=gq: e.scalar_tensor_tensor(
                obuf[bh][:, qb], p_o[:, po], 1.0, rb[:, gq % 2], ALU.mult, ALU.mult), waits=[(SC, sc_rb[gq])], inc=VC)
            vc_ob[po] = VC.n
            vc_stt[gq] = VC.n

        for bh in range(4):
            hl, b = bh % 2, bh // 2
            for qb in range(8):
                npair = 2 * (qb + 1)
                po = qb % 2
                if qb >= 1:
                    emit_evac(bh * 8 + qb - 1)

                def emit_av(j, gp_j, qb=qb, hl=hl, b=b, po=po):
                    last = (j == 2 * (qb + 1) - 1)
                    asl = gp_j % 4
                    for h2 in range(2):
                        i = 2 * j + h2
                        s3.add("tensor", lambda e, hl=hl, b=b, i=i, asl=asl, po=po, h2=h2, sp=(last and h2 == 1):
                               e.matmul(p_o[:, po], vbuf[b][:, hl, i], atr[:, asl, h2], start=(i == 0), stop=sp),
                               waits=[(VC, vc_p2[gp_j])] if h2 == 0 else [])
                        wden = []
                        if i == 0 and vc_dr_prev != 0:
                            pq = vc_dr_prev[1]
                            wden = [(SC, tailq[pq])]
                        s3.add("tensor", lambda e, i=i, asl=asl, h2=h2, st=(i == 0), sp=(last and h2 == 1):
                               e.matmul(p_d[:], ones_c[:], atr[:, asl, h2], start=st, stop=sp),
                               waits=wden, inc=TC if h2 == 1 else None)
                    tc_den[gp_j] = TC.n

                for j in range(npair):
                    pb0 = (2 * j) % 4
                    w = [(SC, sc_sq.get(gp - 2, 0))]
                    if j == 0:
                        w.append((VC, vc_ob[po]))
                    for h2 in range(2):
                        i = 2 * j + h2
                        s3.add("tensor", lambda e, hl=hl, b=b, i=i, qb=qb, pb=pb0 + h2: e.matmul(
                            p_s[:, pb], qkT[b][:, 2 + hl, 128 * i:128 * (i + 1)], qkT[b][:, hl, 512 * qb:512 * (qb + 1)],
                            start=True, stop=True), waits=w if h2 == 0 else [], inc=TC if h2 == 1 else None)
                    tc_st[gp] = TC.n
                    diag = 2 * j >= 4 * qb
                    ps2 = gp % 4
                    wv = [(TC, tc_st[gp]), (VC, vc_p2.get(gp - 4, 0))]
                    if pc_mask.get(gp - 4) is not None:
                        wv.append((PC, pc_mask[gp - 4]))
                    s3.add("scalar", lambda e, pb0=pb0, ps2=ps2: e.activation(
                        s2a[:, ps2].rearrange("p a n -> p (a n)"), p_s[:, pb0:pb0 + 2].rearrange("p a n -> p (a n)"),
                        AF.Square), waits=wv, inc=SC)
                    sc_sq[gp] = SC.n
                    if diag:
                        for h2 in range(2):
                            i = 2 * j + h2
                            base = qb * 512 - i * 128
                            s3.add("gpsimd", lambda e, ps2=ps2, h2=h2, base=base: e.affine_select(
                                out=s2m[:, h2], in_=s2a[:, ps2, h2], pattern=[[1, 512]],
                                compare_op=ALU.is_ge, fill=0.0, base=base, channel_multiplier=-1),
                                waits=[(SC, sc_sq[gp]), (VC, s2m_user)] if h2 == 0 else [], inc=PC if h2 == 1 else None)
                        pc_mask[gp] = PC.n
                    asl = gp % 4
                    w = [(TC, tc_den.get(gp - 4, 0))]
                    if diag:
                        w.append((PC, pc_mask[gp]))
                        sv = s2m.rearrange("p a n -> p (a n)")
                    else:
                        w.append((SC, sc_sq[gp]))
                        sv = s2a[:, ps2].rearrange("p a n -> p (a n)")
                    s3.add("vector", lambda e, sv=sv, asl=asl: e.tensor_tensor(
                        atr[:, asl].rearrange("p a n -> p (a n)"), sv, sv, ALU.mult), waits=w, inc=VC)
                    vc_p2[gp] = VC.n
                    if diag:
                        s2m_user = VC.n
                    if j >= 3:
                        emit_av(j - 3, gp - 3)
                    gp += 1
                    if qb >= 1:
                        if j == 1:
                            emit_recip(bh * 8 + qb - 1)
                        if j == npair - 2:
                            emit_tail(bh, qb - 1)
                for jf in (3, 2, 1):
                    if npair - jf >= 0:
                        emit_av(npair - jf, gp - jf)
                tc_den_last[bh * 8 + qb] = tc_den[gp - 1]
                vc_dr_prev = ("pending", bh * 8 + qb)
            emit_evac(bh * 8 + 7)
            emit_recip(bh * 8 + 7)
            emit_tail(bh, 7)
            hl2, b2 = bh % 2, bh // 2
            vc_ob_last = max(vc_ob[0], vc_ob[1])
            for jd in range(4 * b2, 4 * b2 + 4):
                nj = (jd % 4) * 1024
                s3.add("sync", lambda e, jd=jd, hl2=hl2, bh=bh, nj=nj: e.dma_start(
                    out=cc3_in[jd * 256 + hl2 * 128: jd * 256 + (hl2 + 1) * 128, :],
                    in_=obuf[bh].rearrange("p q n -> p (q n)")[:, nj:nj + 1024]),
                    waits=[(VC, vc_ob_last)] if jd == 4 * b2 else [], inc=CC3W, inck=16)
            sd_b = CC3W.n
            for jd in range(4 * b2, 4 * b2 + 4):
                s3.add("gpsimd", lambda e, jd=jd, hl2=hl2: e.collective_compute(
                    "AllGather", ALU.bypass,
                    ins=[cc3_in[jd * 256 + hl2 * 128: jd * 256 + hl2 * 128 + 128, :]],
                    outs=[cc3_out[(jd * 2 + hl2) * 1024: (jd * 2 + hl2 + 1) * 1024, :]],
                    replica_groups=RG),
                    waits=[(CC3W, sd_b)] if jd == 4 * b2 else [], inc=CCA3)
        cc_ag3 = CCA3.n

        if "d_ob0" in dbg:
            vcv = VC.n
            for i in range(4):
                s3.add("sync", lambda e, i=i: e.dma_start(out=dbg[f"d_ob{i}"], in_=obuf[i][:]),
                       waits=[(VC, vcv)], inc=DBG, inck=16)
        s3.emit(nc)
    stageB.close()
    if STAGES <= 4:
        _dev_final_barrier()
        stageC.close()
        return nc

    stageC.close()
    wout_sb = sb("wout_sb", [128, KCH, 1024], BF16)
    attnT = sb("attnT", [128, H, ROWS], BF16)
    bias_bc = sb("bias_bc", [128, E], F32)

    s45 = Sched(marks_of(ALLSEMS), allsems=ALLSEMS)
    with nc.sbuf_tensor("wsum_h", [128, KCH, 1024], BF16) as wsum_h, \
         nc.sbuf_tensor("f_full", [128, 2, 8, 2, 512], BF16) as f_full, \
         nc.sbuf_tensor("fb_row", [1, E], F32) as fb_row, \
         nc.sbuf_tensor("x_t", [128, 2, 1024], F32) as x_t, \
         nc.sbuf_tensor("xb_t", [128, 2, 1024], F32) as xb_t, \
         nc.sbuf_tensor("o_t", [128, 2, 1024], F32) as o_t, \
         nc.psum_tensor("p_a", [128, 4, 512], F32) as p_a, \
         nc.psum_tensor("p_f", [128, 4, 512], F32) as p_f:

        for hl in range(2):
            def rd(e, hl=hl):
                rank = e.cc_rank(RG)
                src = cc3_out.rearrange("(j l i d) n -> j l i d n", l=2, i=8, d=128)[ds(rank, 1), hl, :, :, :]
                src = src.rearrange("o i d n -> d (o i) n")
                dst = attnT.rearrange("p (i hl) n -> p i hl n", hl=2)[:, :, hl, :]
                return e.dma_start(out=dst, in_=src)
            s45.add("sync", rd, waits=[(CCA3, cc_ag3)] if hl == 0 else [], inc=AT, inck=16)
        sd_attnT = (AT, AT.n)
        if "d_attnT" in dbg:
            s45.add("sync", lambda e: e.dma_start(out=dbg["d_attnT"], in_=attnT[:]), waits=[sd_attnT], inc=DBG, inck=16)

        s45.add("scalar", lambda e: e.dma_start(out=fb_row[:], in_=fbias[:]), inc=ADC, inck=16)
        ad_fb = ADC.n
        sc_bb = 0
        for cchunk in range(4):
            s45.add("tensor", lambda e, cchunk=cchunk: e.matmul(
                p_f[:, 0], ones_r[:], fb_row[:, 512 * cchunk:512 * (cchunk + 1)], start=True, stop=True),
                waits=[(ADC, ad_fb), (SC, sc_bb)], inc=TC)
            tc_bb = TC.n
            s45.add("scalar", lambda e, cchunk=cchunk: e.activation(
                bias_bc[:, 512 * cchunk:512 * (cchunk + 1)], p_f[:, 0], AF.Copy), waits=[(TC, tc_bb)], inc=SC)
            sc_bb = SC.n

        ad_ws = {}
        sc_ff = {}
        tc_ff = {}
        tc_pa_d = {}
        sd_xt5 = {}
        vc_xb = {}
        vc_out = {}
        sd_out = {}
        wo_mark = {}

        def load_wsum(Hh, wtc):
            for rr in range(4):
                waw = [(CCA2, cc_ag2), (TC, wtc)]
                s45.add("scalar", lambda e, Hh=Hh, rr=rr: e.dma_start(
                    out=wsum_h.rearrange("p k (r c) -> p k r c", c=256)[:, :, rr, :],
                    in_=cc2_out[(Hh * 4 + rr) * E:(Hh * 4 + rr + 1) * E, :].rearrange("(k p) c -> p k c", p=128)),
                    waits=waw if rr == 0 else [], inc=WS, inck=16)
            ad_ws[Hh] = WS.n

        def load_wout(Hh, wtc):
            s45.add("gpsimd", lambda e, Hh=Hh: e.dma_start(
                out=wout_sb[:], in_=wout.rearrange("(k p) n -> p k n", p=128)[:, :, 1024 * Hh:1024 * (Hh + 1)]),
                waits=[(TC, wtc)], inc=WO, inck=16)
            wo_mark[Hh] = (WO, WO.n)

        def phaseA(Hh):
            for rc in range(8):
                for cb2 in range(2):
                    pslot = (rc % 2) * 2 + cb2
                    prev = sc_ff.get((Hh, rc - 2, cb2), sc_ff.get((Hh - 1, 6 + (rc % 2), cb2), 0))
                    wf = [(WS, ad_ws[Hh]), (SC, prev if prev else sc_bb)]
                    if pslot == 0:
                        wf.append((SC, sc_bb))
                    for k in range(KCH):
                        s45.add("tensor", lambda e, k=k, rc=rc, cb2=cb2, pslot=pslot: e.matmul(
                            p_f[:, pslot], xnT_own[:, k, 128 * rc:128 * (rc + 1)], wsum_h[:, k, 512 * cb2:512 * (cb2 + 1)],
                            start=(k == 0), stop=(k == KCH - 1)), waits=wf if k == 0 else [], inc=TC if k == KCH - 1 else None)
                    tc_ff[(Hh, rc, cb2)] = TC.n
                    s45.add("scalar", lambda e, Hh=Hh, rc=rc, cb2=cb2, pslot=pslot: e.activation(
                        f_full[:, Hh, rc, cb2], p_f[:, pslot], AF.Identity, scale=SQ8),
                        waits=[(TC, tc_ff[(Hh, rc, cb2)])], inc=SC)
                    sc_ff[(Hh, rc, cb2)] = SC.n

        def phaseB(Hh):
            for rc in range(8):
                xs = rc % 2
                w = [(VC, vc_xb.get((Hh, rc - 2), 0)), (VC, vc_out.get((Hh - 1, rc, 0), 0)), (VC, vc_out.get((Hh - 1, rc, 1), 0))]
                x5sem = X50 if xs == 0 else X51
                s45.add("sync", lambda e, rc=rc, Hh=Hh, xs=xs: e.dma_start(
                    out=x_t[:, xs], in_=x_rows[128 * rc:128 * (rc + 1), 1024 * Hh:1024 * (Hh + 1)]),
                    waits=w, inc=x5sem, inck=16)
                sd_xt5[(Hh, rc)] = (x5sem, x5sem.n)
                s45.add("vector", lambda e, xs=xs, Hh=Hh: e.tensor_tensor(
                    xb_t[:, xs], x_t[:, xs], bias_bc[:, 1024 * Hh:1024 * (Hh + 1)], ALU.add),
                    waits=[sd_xt5[(Hh, rc)], (SC, sc_bb)], inc=VC)
                vc_xb[(Hh, rc)] = VC.n
                for cb2 in range(2):
                    pslot = (rc % 2) * 2 + cb2
                    idx = (Hh, rc, cb2)
                    wa = [sd_attnT, (VC, vc_out.get((Hh, rc - 2, cb2), vc_out.get((Hh - 1, 6 + (rc % 2), cb2), 0))), wo_mark[Hh]]
                    for k in range(KCH):
                        s45.add("tensor", lambda e, k=k, rc=rc, cb2=cb2, pslot=pslot: e.matmul(
                            p_a[:, pslot], attnT[:, k, 128 * rc:128 * (rc + 1)], wout_sb[:, k, 512 * cb2:512 * (cb2 + 1)],
                            start=(k == 0), stop=(k == KCH - 1)), waits=wa if k == 0 else [], inc=TC if k == KCH - 1 else None)
                    tc_pa_d[idx] = TC.n
                    otg = sd_out.get((Hh, rc - 2), sd_out.get((Hh - 1, 6 + (rc % 2)), None))
                    w = [(TC, tc_pa_d[idx]), (SC, sc_ff[idx]), (VC, vc_xb[(Hh, rc)])]
                    if otg is not None:
                        w.append(otg)
                    s45.add("vector", lambda e, Hh=Hh, rc=rc, pslot=pslot, xs=xs, cb2=cb2: e.scalar_tensor_tensor(
                        o_t[:, xs, 512 * cb2:512 * (cb2 + 1)], p_a[:, pslot], 1.0, f_full[:, Hh, rc, cb2], ALU.mult, ALU.add),
                        waits=w)
                    s45.add("vector", lambda e, pslot=pslot, xs=xs, cb2=cb2: e.tensor_tensor(
                        o_t[:, xs, 512 * cb2:512 * (cb2 + 1)], o_t[:, xs, 512 * cb2:512 * (cb2 + 1)],
                        xb_t[:, xs, 512 * cb2:512 * (cb2 + 1)], ALU.add), inc=VC)
                    vc_out[idx] = VC.n
                w = [(VC, vc_out[(Hh, rc, 0)]), (VC, vc_out[(Hh, rc, 1)])]
                otsem = OT0 if xs == 0 else OT1
                s45.add("sync", lambda e, Hh=Hh, rc=rc, xs=xs: e.dma_start(
                    out=out_rows[128 * rc:128 * (rc + 1), 1024 * Hh:1024 * (Hh + 1)], in_=o_t[:, xs]),
                    waits=w, inc=otsem, inck=16)
                sd_out[(Hh, rc)] = (otsem, otsem.n)

        load_wsum(0, 0)
        load_wout(0, 0)
        phaseA(0)
        if STAGES <= 5:
            s45.emit(nc)
            _dev_final_barrier()
            return nc
        load_wsum(1, tc_ff[(0, 7, 1)])
        phaseB(0)
        load_wout(1, tc_pa_d[(0, 7, 1)])
        phaseA(1)
        phaseB(1)
        s45.add("sync", lambda e: e.nop(), waits=[(OT0, OT0.n), (OT1, OT1.n), (DBG, DBG.n)])
        s45.emit(nc)

    return nc

    return nc


def _host_inputs(x, w_qkv, w_out, ln_scale, ln_bias, factors, fbias):
    x2 = np.ascontiguousarray(np.asarray(x, np.float32).reshape(NB * SEQ, E))
    w_qkv = np.asarray(w_qkv, np.float32)
    w_out = np.ascontiguousarray(np.asarray(w_out, np.float32))
    factors = np.asarray(factors, np.float32)
    pos = np.arange(SEQ, dtype=np.float64)[:, None]
    inv = 1.0 / (10000.0 ** (np.arange(0, D, 2, dtype=np.float64) / D))
    ang = pos * inv[None, :]
    rcos = np.ascontiguousarray(np.cos(ang).astype(np.float32))
    rsin = np.ascontiguousarray(np.sin(ang).astype(np.float32))
    ident_f = np.eye(128, dtype=np.float32)
    onescol = np.ones((128, 1), dtype=bfnp)
    onesrow = np.ones((1, 128), dtype=np.float32)
    in_maps = []
    for c in range(NC):
        h0, h1 = 2 * c, 2 * c + 1
        cols = []
        for base in (0, E, 2 * E):
            cols.append(w_qkv[:, base + 128 * h0: base + 128 * (h0 + 1)])
            cols.append(w_qkv[:, base + 128 * h1: base + 128 * (h1 + 1)])
        wq = np.ascontiguousarray(np.concatenate(cols, axis=1))
        in_maps.append({
            "x_rows": np.ascontiguousarray(x2[ROWS * c: ROWS * (c + 1)]),
            "wqkv": wq,
            "wout": w_out,
            "facs": np.ascontiguousarray(factors[:, :, 256 * c: 256 * (c + 1)]),
            "fbias": np.ascontiguousarray(np.asarray(fbias, np.float32).reshape(1, E)),
            "lscale": np.ascontiguousarray(np.asarray(ln_scale, np.float32).reshape(D, 1)),
            "lbias": np.ascontiguousarray(np.asarray(ln_bias, np.float32).reshape(D, 1)),
            "rcos": rcos.astype(bfnp), "rsin": rsin.astype(bfnp),
            "constv": np.tile(np.array([[EPS, SQ8, 0.0, 0.0]], dtype=np.float32), (128, 1)),
            "ident_f": ident_f, "ident_b": ident_f.astype(bfnp), "onescol": onescol, "onesrow": onesrow,
        })
    return in_maps


_RESULTS = {}


def kernel(L0_attn_x, L0_attn_w_qkv, L0_attn_w_out, L0_attn_ln_scale,
           L0_attn_ln_bias, L0_ffn_factors, L0_ffn_bias):
    in_maps = _host_inputs(L0_attn_x, L0_attn_w_qkv, L0_attn_w_out,
                           L0_attn_ln_scale, L0_attn_ln_bias,
                           L0_ffn_factors, L0_ffn_bias)
    with ExitStack() as stack:
        nc = build(stack)
        res = run_bass_kernel_spmd(nc, in_maps, core_ids=list(range(NC)))
    _RESULTS["res"] = res
    out = np.concatenate([res.results[c]["out_rows"] for c in range(NC)], axis=0)
    return out.reshape(NB, SEQ, E)


# version 23
# speedup vs baseline: 1.1292x; 1.1292x over previous
import math
from contextlib import ExitStack
import numpy as np
import ml_dtypes
import concourse.bass as bass
import concourse.mybir as mybir
from concourse.bass import ds
from concourse.bass_utils import run_bass_kernel_spmd
from concourse.replica_groups import maybe_share_collective_output_space

F32 = mybir.dt.float32
BF16 = mybir.dt.bfloat16
AF = mybir.ActivationFunctionType
ALU = mybir.AluOpType
bfnp = ml_dtypes.bfloat16

NC = 8
NB = 2
SEQ = 4096
E = 2048
H = 16
D = 128
ROWS = 1024
KCH = 16
RG = [list(range(NC))]
EPS = 1e-5
SQ8 = 1.0 / math.sqrt(8.0)

DEBUG = set()
STAGES = 9
PARANOID = ""


class Sem:
    def __init__(self, h, name=""):
        self.h = h
        self.n = 0
        self.name = name


class Step:
    __slots__ = ("eng", "fn", "waits", "incs")
    def __init__(self, eng, fn, waits, incs):
        self.eng, self.fn, self.waits, self.incs = eng, fn, waits, incs


class Sched:
    def __init__(self, marks=(), allsems=None):
        self.steps = []
        self.marks = list(marks)
        self._seen = set()
        self.allsems = allsems

    SELF = {"vector": "VC", "scalar": "SC", "gpsimd": "PC"}

    def mark_section(self, marks):
        self._sec_marks = list(marks)
        self._sec_seen = set()

    def add(self, eng, fn, waits=(), inc=None, inck=1):
        waits = list(waits)
        if self.allsems is not None:
            nm = Sched.SELF.get(eng)
            if nm is not None:
                waits += [(x, x.n) for x in self.allsems if x.name == nm]
            if PARANOID and (PARANOID == "all" or eng in PARANOID):
                waits += [(x, x.n) for x in self.allsems]
        w = [(s, v) for (s, v) in waits if v]
        if eng not in self._seen:
            self._seen.add(eng)
            w = self.marks + w
        sm = getattr(self, "_sec_marks", None)
        if sm is not None and eng not in self._sec_seen:
            self._sec_seen.add(eng)
            w = [(x, v) for (x, v) in sm if v] + w
        incs = []
        if inc is not None:
            incs.append((inc, inck))
            inc.n += inck
        self.steps.append(Step(eng, fn, w, incs))
        return inc.n if inc is not None else None

    def emit(self, nc):
        with nc.Block() as block:
            for name in ("sync", "scalar", "vector", "tensor", "gpsimd"):
                mine = [s for s in self.steps if s.eng == name]
                if not mine:
                    continue
                def section(eng, mine=mine):
                    for st in mine:
                        seen = {}
                        for (sem, val) in st.waits:
                            if seen.get(id(sem), -1) < val:
                                seen[id(sem)] = val
                                eng.wait_ge(sem.h, val)
                        inst = st.fn(eng)
                        for (sem, k) in st.incs:
                            inst.then_inc(sem.h, k)
                getattr(block, name)(section)


def marks_of(sems):
    return [(s, s.n) for s in sems if s.n > 0]


def build(stack):
    nc = bass.Bass("TRN2", target_bir_lowering=False, debug=False)

    x_rows = nc.dram_tensor("x_rows", [ROWS, E], F32, kind="ExternalInput").ap()
    wqkv = nc.dram_tensor("wqkv", [E, 768], F32, kind="ExternalInput").ap()
    wout = nc.dram_tensor("wout", [E, E], F32, kind="ExternalInput").ap()
    facs = nc.dram_tensor("facs", [8, E, 256], F32, kind="ExternalInput").ap()
    fbias = nc.dram_tensor("fbias", [1, E], F32, kind="ExternalInput").ap()
    lscale = nc.dram_tensor("lscale", [D, 1], F32, kind="ExternalInput").ap()
    lbias = nc.dram_tensor("lbias", [D, 1], F32, kind="ExternalInput").ap()
    rcos = nc.dram_tensor("rcos", [SEQ, 64], BF16, kind="ExternalInput").ap()
    rsin = nc.dram_tensor("rsin", [SEQ, 64], BF16, kind="ExternalInput").ap()
    ident_f = nc.dram_tensor("ident_f", [128, 128], F32, kind="ExternalInput").ap()
    ident_b = nc.dram_tensor("ident_b", [128, 128], BF16, kind="ExternalInput").ap()
    onescol = nc.dram_tensor("onescol", [128, 1], BF16, kind="ExternalInput").ap()
    onesrow = nc.dram_tensor("onesrow", [1, 128], F32, kind="ExternalInput").ap()
    constv = nc.dram_tensor("constv", [128, 4], F32, kind="ExternalInput").ap()
    out_rows = nc.dram_tensor("out_rows", [ROWS, E], F32, kind="ExternalOutput").ap()

    sh_ag = maybe_share_collective_output_space("AllGather", RG)
    cc1_in = [nc.dram_tensor(f"cc1_in{h}", [E, ROWS // 2], BF16, kind="Internal").ap() for h in range(2)]
    cc1_out = [nc.dram_tensor(f"cc1_out{h}", [NC * E, ROWS // 2], BF16, kind="Internal", addr_space=sh_ag).ap() for h in range(2)]
    cc2_in = nc.dram_tensor("cc2_in", [E, 256], BF16, kind="Internal").ap()
    cc2_out = nc.dram_tensor("cc2_out", [NC * E, 256], BF16, kind="Internal", addr_space=sh_ag).ap()
    cc3_in = nc.dram_tensor("cc3_in", [E, ROWS], BF16, kind="Internal").ap()
    cc3_out = nc.dram_tensor("cc3_out", [NC * E, ROWS], BF16, kind="Internal", addr_space=sh_ag).ap()

    dbg = {}
    for name, shape, dt in (
        ("d_xnT", [128, KCH, ROWS], BF16),
        ("d_qkT0", [128, 4, SEQ], BF16),
        ("d_qkT1", [128, 4, SEQ], BF16),
        ("d_v0", [128, 2, 32, 128], BF16),
        ("d_v1", [128, 2, 32, 128], BF16),
        ("d_ob0", [128, 8, 512], BF16),
        ("d_ob1", [128, 8, 512], BF16),
        ("d_ob2", [128, 8, 512], BF16),
        ("d_ob3", [128, 8, 512], BF16),
        ("d_attnT", [128, H, ROWS], BF16),
        ("d_wsum", [128, KCH, 1024], BF16),
        ("d_wout", [128, KCH, 1024], BF16),
        ("d_cc2", [NC * E, 256], BF16),
        ("d_wtmp", [128, KCH, 256], BF16),
        ("d_faca", [128, KCH, 256], F32),
    ):
        if name in DEBUG:
            dbg[name] = nc.dram_tensor(name, shape, dt, kind="ExternalOutput").ap()

    def mksem(name):
        return Sem(stack.enter_context(nc.semaphore(name)), name)
    TC, VC, SC, PC = mksem("TC"), mksem("VC"), mksem("SC"), mksem("PC")
    SDX = mksem("SDX")
    ADC = mksem("ADC")
    WQ = mksem("WQ")
    FA = mksem("FA")
    FB0, FB1, FB2 = mksem("FB0"), mksem("FB1"), mksem("FB2")
    CC1W, CC2W, CC3W = mksem("CC1W"), mksem("CC2W"), mksem("CC3W")
    CCA1, CCA2, CCA3 = mksem("CCA1"), mksem("CCA2"), mksem("CCA3")
    XB0, XB1 = mksem("XB0"), mksem("XB1")
    WO = mksem("WO")
    WS = mksem("WS")
    AT = mksem("AT")
    OT0, OT1 = mksem("OT0"), mksem("OT1")
    X50, X51 = mksem("X50"), mksem("X51")
    DBG = mksem("DBG")
    ALLSEMS = (TC, VC, SC, PC, SDX, ADC, WQ, FA, FB0, FB1, FB2, CC1W, CC2W, CC3W,
               CCA1, CCA2, CCA3, XB0, XB1, WO, WS, AT, OT0, OT1, X50, X51, DBG)

    def _dev_final_barrier():
        sf = Sched({}, allsems=ALLSEMS)
        sf.add("sync", lambda e: e.nop(), waits=[(S, S.n) for S in ALLSEMS if S.n > 0])
        sf.emit(nc)


    sb = lambda n, s, d: stack.enter_context(nc.sbuf_tensor(n, s, d))

    id_f = sb("id_f", [128, 128], F32)
    id_b = sb("id_b", [128, 128], BF16)
    ones_c = sb("ones_c", [128, 1], BF16)
    ones_r = sb("ones_r", [1, 128], F32)
    ls_c = sb("ls_c", [128, 1], F32)
    lb_c = sb("lb_c", [128, 1], F32)
    xnT_own = sb("xnT_own", [128, KCH, ROWS], BF16)
    cst = sb("cst", [128, 4], F32)
    nc.const_aps.register(F32, EPS, cst[:, 0:1])
    nc.const_aps.register(F32, SQ8, cst[:, 1:2])

    stageA = ExitStack()
    sbA = lambda n, s, d: stageA.enter_context(nc.sbuf_tensor(n, s, d, side="right"))
    cos_sb = sbA("cos_sb", [128, 32, 64], BF16)
    sin_sb = sbA("sin_sb", [128, 32, 64], BF16)
    wqkv_sb = sbA("wqkv_sb", [128, KCH, 768], BF16)

    s1 = Sched(allsems=ALLSEMS)
    s1.add("scalar", lambda e: e.dma_start(out=cos_sb[:], in_=rcos.rearrange("(t p) f -> p t f", p=128)), inc=ADC, inck=16)
    s1.add("scalar", lambda e: e.dma_start(out=sin_sb[:], in_=rsin.rearrange("(t p) f -> p t f", p=128)), inc=ADC, inck=16)
    s1.add("scalar", lambda e: e.dma_start(out=id_f[:], in_=ident_f[:]), inc=ADC, inck=16)
    s1.add("scalar", lambda e: e.dma_start(out=id_b[:], in_=ident_b[:]), inc=ADC, inck=16)
    s1.add("scalar", lambda e: e.dma_start(out=ones_c[:], in_=onescol[:]), inc=ADC, inck=16)
    s1.add("scalar", lambda e: e.dma_start(out=ones_r[:], in_=onesrow[:]), inc=ADC, inck=16)
    s1.add("scalar", lambda e: e.dma_start(out=ls_c[:], in_=lscale[:]), inc=ADC, inck=16)
    s1.add("scalar", lambda e: e.dma_start(out=lb_c[:], in_=lbias[:]), inc=ADC, inck=16)
    s1.add("scalar", lambda e: e.dma_start(out=cst[:], in_=constv[:]), inc=ADC, inck=16)
    ad_consts = ADC.n

    sbR = lambda n, sh, d: nc.sbuf_tensor(n, sh, d, side="right")
    with nc.sbuf_tensor("x_sb", [128, 2, E], F32) as x_sb, \
         nc.sbuf_tensor("st1", [128, 24], F32) as st1, \
         nc.sbuf_tensor("mv1", [128, 2], F32) as mv1, \
         nc.sbuf_tensor("rstd", [128, 1], F32) as rstd, \
         nc.sbuf_tensor("nbias", [128, 1], F32) as nbias, \
         nc.sbuf_tensor("sdv", [128, 1], F32) as sdv, \
         nc.sbuf_tensor("xnbf", [128, 2, E], BF16) as xnbf, \
         sbR("faca", [128, KCH, 256], F32) as faca, \
         sbR("facb", [128, 3, KCH, 256], F32) as facb, \
         nc.psum_tensor("tp1", [128, 2, 1024], BF16) as tp1:

        s1.add("gpsimd", lambda e: e.dma_start(out=wqkv_sb[:], in_=wqkv.rearrange("(k p) c -> p k c", p=128)), inc=WQ, inck=16)
        wq_mark = (WQ, WQ.n)
        fbsem = {0: FB0, 1: FB1, 2: FB2}
        ad_fac = {}
        pc_add = {0: 0}

        def fac_dma(r):
            if r == 0:
                s1.add("gpsimd", lambda e: e.dma_start(out=faca[:], in_=facs[0].rearrange("(k p) c -> p k c", p=128)), inc=FA, inck=16)
                return
            slot = (r - 1) % 3
            w = [(PC, pc_add.get(r - 3, 0))]
            s1.add("gpsimd", lambda e, r=r, slot=slot: e.dma_start(
                out=facb[:, slot], in_=facs[r].rearrange("(k p) c -> p k c", p=128)), waits=w, inc=fbsem[slot], inck=16)
            ad_fac[r] = (fbsem[slot], fbsem[slot].n)

        x5sems = {0: X50, 1: X51}
        sd_xt = {}
        sc_applyd = {}

        def x_dma(t):
            xs = t % 2
            w = [(SC, sc_applyd.get(t - 2, 0))]
            s1.add("sync", lambda e, t=t, xs=xs: e.dma_start(
                out=x_sb[:, xs], in_=x_rows[128 * t:128 * (t + 1), :]), waits=w, inc=x5sems[xs], inck=16)
            sd_xt[t] = (x5sems[xs], x5sems[xs].n)

        x_dma(0)
        x_dma(1)
        vc_evac = {0: 0, 1: 0}
        tc_xn = {0: 0, 1: 0}
        sc_prev_apply = 0
        for t in range(8):
            xs = t % 2
            for s4 in range(4):
                w = [sd_xt[t], (VC, VC.n), (SC, SC.n)] if s4 == 0 else []
                s1.add("vector", lambda e, xs=xs, s4=s4: e.bn_stats(st1[:, 6 * s4:6 * s4 + 6], x_sb[:, xs, 512 * s4:512 * (s4 + 1)]),
                       waits=w, inc=VC if s4 == 3 else None)
            vc_st = VC.n
            s1.add("vector", lambda e: e.bn_aggr(mv1[:], st1[:]), waits=[(VC, vc_st)], inc=VC)
            vc_mv = VC.n
            s1.add("scalar", lambda e: e.activation(sdv[:], mv1[:, 1:2], AF.Sqrt, bias=EPS),
                   waits=[(VC, vc_mv), (ADC, ad_consts)], inc=SC)
            sc_sd = SC.n
            s1.add("vector", lambda e: e.reciprocal(rstd[:], sdv[:]), waits=[(SC, max(sc_sd, sc_prev_apply))], inc=VC)
            vc_r = VC.n
            s1.add("vector", lambda e: e.tensor_scalar(nbias[:], mv1[:, 0:1], rstd[:], -1.0, ALU.mult, ALU.mult),
                   waits=[(VC, vc_r)], inc=VC)
            vc_nb = VC.n
            s1.add("scalar", lambda e, xs=xs: e.activation(xnbf[:, xs], x_sb[:, xs], AF.Identity, bias=nbias[:], scale=rstd[:]),
                   waits=[(VC, vc_nb), (TC, tc_xn[xs])], inc=SC)
            sc_apply = SC.n
            sc_prev_apply = sc_apply
            sc_applyd[t] = sc_apply
            if t + 2 < 8:
                x_dma(t + 2)
            for g4 in range(4):
                bank = g4 % 2
                for j in range(4):
                    kc = g4 * 4 + j
                    w = [(SC, sc_apply), (VC, vc_evac[bank]), (ADC, ad_consts)] if j == 0 else []
                    s1.add("tensor", lambda e, xs=xs, kc=kc, bank=bank, j=j: e.transpose(
                        tp1[:, bank, 128 * j:128 * (j + 1)], xnbf[:, xs, 128 * kc:128 * (kc + 1)], id_b[:]),
                        waits=w, inc=TC if j == 3 else None)
                tc_tp = TC.n
                if g4 == 3:
                    tc_xn[xs] = tc_tp
                s1.add("vector", lambda e, t=t, g4=g4, bank=bank: e.tensor_copy(
                    xnT_own[:, 4 * g4:4 * g4 + 4, 128 * t:128 * (t + 1)],
                    tp1[:, bank, 0:512].rearrange("p (j n) -> p j n", n=128)),
                    waits=[(TC, tc_tp)], inc=VC)
                vc_evac[bank] = VC.n
            if t == 3:
                vc_half = VC.n
                s1.add("sync", lambda e: e.dma_start(
                    out=cc1_in[0].rearrange("(k p) n -> p k n", p=128),
                    in_=xnT_own[:, :, 0:512]), waits=[(VC, vc_half)], inc=CC1W, inck=16)
                cc1w_a = CC1W.n
        vc_xnT = VC.n
        sc_LN_end = SC.n
        tc_s1_end = TC.n
        s1.add("sync", lambda e: e.dma_start(
            out=cc1_in[1].rearrange("(k p) n -> p k n", p=128),
            in_=xnT_own[:, :, 512:1024]), waits=[(VC, vc_xnT)], inc=CC1W, inck=16)
        cc1w_b = CC1W.n
        for r in range(4):
            fac_dma(r)
        for r in range(1, 8):
            w = [ad_fac[r], (PC, pc_add[r - 1])]
            if r == 1:
                w.append((FA, 16))
            s1.add("gpsimd", lambda e, slot=(r - 1) % 3: e.tensor_tensor(faca[:], faca[:], facb[:, slot], ALU.add),
                   waits=w, inc=PC)
            pc_add[r] = PC.n
            if r + 3 <= 7:
                fac_dma(r + 3)
            if r == 4:
                s1.add("gpsimd", lambda e: e.collective_compute("AllGather", ALU.bypass, ins=[cc1_in[0]], outs=[cc1_out[0]], replica_groups=RG),
                       waits=[(CC1W, cc1w_a)], inc=CCA1)
                cc_ag1a = CCA1.n
        s1.add("gpsimd", lambda e: e.collective_compute("AllGather", ALU.bypass, ins=[cc1_in[1]], outs=[cc1_out[1]], replica_groups=RG),
               waits=[(CC1W, cc1w_b)], inc=CCA1)
        cc_ag1b = CCA1.n
        if "d_xnT" in dbg:
            s1.add("sync", lambda e: e.dma_start(out=dbg["d_xnT"], in_=xnT_own[:]), waits=[(VC, vc_xnT)], inc=DBG, inck=16)
        cc_ag1 = cc_ag1b
        s1.add("gpsimd", lambda e: e.dma_start(out=cc2_in.rearrange("(k p) c -> p k c", p=128), in_=faca[:]),
               waits=[(PC, pc_add[7])], inc=CC2W, inck=16)
        sd_cc2 = CC2W.n
        s1.add("gpsimd", lambda e: e.collective_compute("AllGather", ALU.bypass, ins=[cc2_in], outs=[cc2_out], replica_groups=RG),
               waits=[(CC2W, sd_cc2)], inc=CCA2)
        cc_ag2 = CCA2.n
        if STAGES <= 2:
            s1.emit(nc)
    if STAGES <= 2:
        _dev_final_barrier()
        stageA.close()
        return nc

    stageB = ExitStack()
    sbB = lambda n, s, d: stageB.enter_context(nc.sbuf_tensor(n, s, d))
    qkT = [sbB(f"qkT{b}", [128, 4, SEQ], BF16) for b in range(NB)]
    vbuf = [sbB(f"vbuf{b}", [128, 2, 32, 128], BF16) for b in range(NB)]

    s2_marks = [(VC, vc_xnT), (SC, sc_LN_end), (TC, tc_s1_end), wq_mark]
    s2 = s1
    s2.mark_section(s2_marks)
    with nc.sbuf_tensor("xbuf", [128, 2, KCH, 256], BF16) as xbuf, \
         nc.sbuf_tensor("qk_r", [128, 3, 4, 128], F32) as qk_r, \
         nc.sbuf_tensor("tm_c", [128, 4, 2, 64], F32) as tm_c, \
         nc.sbuf_tensor("tm_s", [128, 4, 2, 64], F32) as tm_s, \
         nc.sbuf_tensor("hst", [128, 24], F32) as hst, \
         nc.sbuf_tensor("hmv", [128, 4, 2], F32) as hmv, \
         nc.sbuf_tensor("hsd", [128, 4], F32) as hsd, \
         nc.sbuf_tensor("hr", [128, 4], F32) as hr, \
         nc.sbuf_tensor("hnb", [128, 4], F32) as hnb, \
         nc.sbuf_tensor("u_n", [128, 4, 4, 128], F32) as u_n, \
         nc.psum_tensor("p_qk", [128, 3, 512], F32) as p_qk, \
         nc.psum_tensor("p_v", [128, 2, 512], F32) as p_v, \
         nc.psum_tensor("p_tp", [128, 2, 512], F32) as p_tp:

        sd_blk = {}
        tc_blkdone = {}
        vc_ropepsum = {}
        sc_vdone = {}
        sc_tpevac = {0: 0, 1: 0}
        tc_tpdone = {}
        sc_undone = {0: 0, 1: 0, 2: 0}
        sc_un_latest = [0]
        sc_unr = {}

        NBLK = 32

        xbsem = {0: XB0, 1: XB1}

        def xbuf_dma(blk):
            r, q4 = blk // 4, blk % 4
            slot = blk % 2
            half = q4 // 2
            w = [(CCA1, cc_ag1a if half == 0 else cc_ag1b), (TC, tc_blkdone.get(blk - 2, 0))]
            s2.add("sync", lambda e, r=r, q4=q4, slot=slot, half=half: e.dma_start(
                out=xbuf[:, slot],
                in_=cc1_out[half][r * E:(r + 1) * E, 256 * (q4 % 2):256 * (q4 % 2 + 1)].rearrange("(k p) n -> p k n", p=128)),
                waits=w, inc=xbsem[slot], inck=16)
            sd_blk[blk] = (xbsem[slot], xbsem[slot].n)

        xbuf_dma(0)
        xbuf_dma(1)

        def emit_tp(rcg2):
            u4 = rcg2 % 4
            tpb = rcg2 % 2
            blk2 = rcg2 // 2
            r2, q42 = blk2 // 4, blk2 % 4
            b2 = r2 // 4
            nch2 = ((r2 % 4) * 1024 + q42 * 256 + (rcg2 % 2) * 128) // 128
            for g in range(4):
                w = [(SC, max(sc_unr[rcg2], sc_tpevac[tpb]))] if g == 0 else []
                s2.add("tensor", lambda e, g=g, u4=u4, tpb=tpb: e.transpose(
                    p_tp[:, tpb, 128 * g:128 * (g + 1)], u_n[:, u4, g], id_f[:]),
                    waits=w, inc=TC if g == 3 else None)
            tc_tp = TC.n
            tc_tpdone[rcg2] = tc_tp
            qkT_dst = qkT[b2].rearrange("p g (c n) -> p g c n", n=128)[:, :, nch2]
            s2.add("scalar", lambda e, qkT_dst=qkT_dst, tpb=tpb: e.activation(
                qkT_dst, p_tp[:, tpb].rearrange("p (g n) -> p g n", n=128),
                AF.Identity, bias=lb_c[:], scale=ls_c[:]), waits=[(TC, tc_tp)], inc=SC)
            sc_tpevac[tpb] = SC.n

        for blk in range(NBLK):
            r, q4 = blk // 4, blk % 4
            slot = blk % 2
            b = r // 4
            for rc in range(2):
                rcg = blk * 2 + rc
                n0 = (r % 4) * 1024 + q4 * 256 + rc * 128
                nch = n0 // 128
                pb = rcg % 3
                w = [sd_blk[blk], (SC, sc_vdone.get(rcg - 2, 0)), wq_mark,
                     (VC, vc_ropepsum.get(rcg - 3, 0))]
                for k in range(KCH):
                    s2.add("tensor", lambda e, slot=slot, k=k, rc=rc, pb=pb: e.matmul(
                        p_qk[:, pb], xbuf[:, slot, k, 128 * rc:128 * (rc + 1)], wqkv_sb[:, k, 0:512],
                        start=(k == 0), stop=(k == KCH - 1)), waits=w if k == 0 else [])
                pv2 = rcg % 2
                for k in range(KCH):
                    s2.add("tensor", lambda e, slot=slot, k=k, rc=rc, pv2=pv2: e.matmul(
                        p_v[:, pv2, 0:256], xbuf[:, slot, k, 128 * rc:128 * (rc + 1)], wqkv_sb[:, k, 512:768],
                        start=(k == 0), stop=(k == KCH - 1)), inc=TC if k == KCH - 1 else None)
                tc_qkv = TC.n
                if rc == 1:
                    tc_blkdone[blk] = tc_qkv
                pq4 = p_qk[:, pb].rearrange("p (g t f) -> p g t f", t=2, f=64)
                cb = cos_sb[:, nch].unsqueeze(1).unsqueeze(2).broadcast_to([128, 4, 2, 64])
                sbn = sin_sb[:, nch].unsqueeze(1).unsqueeze(2).broadcast_to([128, 4, 2, 64])
                s2.add("vector", lambda e, pq4=pq4, cb=cb: e.tensor_tensor(tm_c[:], pq4, cb, ALU.mult),
                       waits=[(TC, tc_qkv)])
                s2.add("vector", lambda e, pq4=pq4, sbn=sbn: e.tensor_tensor(tm_s[:], pq4, sbn, ALU.mult), inc=VC)
                vc_m = VC.n
                vc_ropepsum[rcg] = vc_m
                qr4 = qk_r[:, pb].rearrange("p g (t f) -> p g t f", t=2, f=64)
                s2.add("vector", lambda e, qr4=qr4: e.tensor_tensor(qr4[:, :, 0], tm_c[:, :, 0], tm_s[:, :, 1], ALU.subtract),
                       waits=[(VC, vc_m), (SC, sc_undone[pb])])
                s2.add("vector", lambda e, qr4=qr4: e.tensor_tensor(qr4[:, :, 1], tm_s[:, :, 0], tm_c[:, :, 1], ALU.add), inc=VC)
                vc_rope = VC.n
                for g in range(4):
                    w = [(VC, vc_rope), (VC, VC.n), (SC, SC.n)] if g == 0 else []
                    s2.add("vector", lambda e, g=g, pb=pb: e.bn_stats(hst[:, 6 * g:6 * g + 6], qk_r[:, pb, g]),
                           waits=w, inc=VC if g == 3 else None)
                vc_hst = VC.n
                for g in range(4):
                    s2.add("vector", lambda e, g=g: e.bn_aggr(hmv[:, g], hst[:, 6 * g:6 * g + 6]),
                           waits=[(VC, vc_hst)] if g == 0 else [], inc=VC if g == 3 else None)
                vc_hmv = VC.n
                s2.add("scalar", lambda e: e.activation(hsd[:], hmv[:, :, 1], AF.Sqrt, bias=EPS),
                       waits=[(VC, vc_hmv)], inc=SC)
                sc_hsd = SC.n
                s2.add("vector", lambda e: e.reciprocal(hr[:], hsd[:]),
                       waits=[(SC, sc_hsd), (SC, sc_un_latest[0])], inc=VC)
                vc_hr = VC.n
                s2.add("vector", lambda e: e.tensor_scalar(hnb[:], hmv[:, :, 0], -1.0, None, ALU.mult),
                       waits=[(VC, vc_hr)])
                s2.add("vector", lambda e: e.tensor_tensor(hnb[:], hnb[:], hr[:], ALU.mult), inc=VC)
                vc_hnb = VC.n
                u4 = rcg % 4
                for g in range(4):
                    w = [(VC, vc_hnb), (TC, tc_tpdone.get(rcg - 4, 0))] if g == 0 else []
                    s2.add("scalar", lambda e, g=g, pb=pb, u4=u4: e.activation(
                        u_n[:, u4, g], qk_r[:, pb, g], AF.Identity, bias=hnb[:, g:g + 1], scale=hr[:, g:g + 1]),
                        waits=w, inc=SC if g == 3 else None)
                sc_un = SC.n
                sc_undone[pb] = sc_un
                sc_un_latest[0] = sc_un
                sc_unr[rcg] = sc_un
                s2.add("scalar", lambda e, b=b, nch=nch, pv2=pv2: e.activation(
                    vbuf[b][:, :, nch], p_v[:, pv2, 0:256].rearrange("p (h n) -> p h n", n=128), AF.Copy),
                    waits=[(TC, tc_qkv)], inc=SC)
                sc_vdone[rcg] = SC.n
                if rcg >= 2:
                    emit_tp(rcg - 2)
            if blk + 2 < NBLK:
                xbuf_dma(blk + 2)
        emit_tp(NBLK * 2 - 2)
        emit_tp(NBLK * 2 - 1)

        if "d_qkT0" in dbg:
            scv = SC.n
            s2.add("sync", lambda e: e.dma_start(out=dbg["d_qkT0"], in_=qkT[0][:]), waits=[(SC, scv)], inc=DBG, inck=16)
            s2.add("sync", lambda e: e.dma_start(out=dbg["d_qkT1"], in_=qkT[1][:]), inc=DBG, inck=16)
        if "d_v0" in dbg:
            scv = SC.n
            s2.add("sync", lambda e: e.dma_start(out=dbg["d_v0"], in_=vbuf[0][:]), waits=[(SC, scv)], inc=DBG, inck=16)
            s2.add("sync", lambda e: e.dma_start(out=dbg["d_v1"], in_=vbuf[1][:]), inc=DBG, inck=16)
        s2.emit(nc)
    stageA.close()
    if STAGES <= 3:
        _dev_final_barrier()
        stageB.close()
        return nc

    stageC = ExitStack()
    obuf = [stageC.enter_context(nc.sbuf_tensor(f"obuf{i}", [128, 8, 512], BF16, side="right")) for i in range(4)]

    s3 = Sched(marks_of(ALLSEMS), allsems=ALLSEMS)
    with nc.sbuf_tensor("s2a", [128, 4, 2, 512], F32) as s2a, \
         nc.sbuf_tensor("s2m", [128, 2, 512], F32) as s2m, \
         nc.sbuf_tensor("atr", [128, 4, 2, 512], BF16) as atr, \
         nc.sbuf_tensor("dsb", [1, 2, 512], F32) as dsb, \
         nc.sbuf_tensor("dr", [1, 2, 512], F32) as dr, \
         nc.sbuf_tensor("rb", [128, 2, 512], F32) as rb, \
         nc.psum_tensor("p_s", [128, 4, 512], F32) as p_s, \
         nc.psum_tensor("p_o", [128, 2, 512], F32) as p_o, \
         nc.psum_tensor("p_d", [1, 512], F32) as p_d, \
         nc.psum_tensor("p_b", [128, 512], F32) as p_b:

        sc_sq = {}
        pc_mask = {}
        vc_p2 = {}
        tc_den = {}
        tc_st = {}
        s2m_user = 0
        vc_ob = {0: 0, 1: 0}
        vc_dr_prev = 0
        tailq = {}
        gp = 0

        tc_den_last = {}
        vc_recip = {}
        tc_bcst = {}
        sc_rb = {}
        vc_stt = {}

        def emit_evac(gq):
            s3.add("scalar", lambda e, gq=gq: e.activation(dsb[:, gq % 2], p_d[:], AF.Copy),
                   waits=[(TC, tc_den_last[gq]), (VC, vc_recip.get(gq - 2, 0))], inc=SC)
            tailq[gq] = SC.n

        def emit_recip(gq):
            s3.add("vector", lambda e, gq=gq: e.reciprocal(dr[:, gq % 2], dsb[:, gq % 2]),
                   waits=[(SC, tailq[gq]), (TC, tc_bcst.get(gq - 2, 0))], inc=VC)
            vc_recip[gq] = VC.n

        def emit_tail(bh, qb):
            po = qb % 2
            gq = bh * 8 + qb
            s3.add("tensor", lambda e, gq=gq: e.matmul(p_b[:], ones_r[:], dr[:, gq % 2], start=True, stop=True),
                   waits=[(VC, vc_recip[gq]), (SC, sc_rb.get(gq - 1, 0))], inc=TC)
            tc_bcst[gq] = TC.n
            s3.add("scalar", lambda e, gq=gq: e.activation(rb[:, gq % 2], p_b[:], AF.Copy),
                   waits=[(TC, tc_bcst[gq]), (VC, vc_stt.get(gq - 2, 0))], inc=SC)
            sc_rb[gq] = SC.n
            s3.add("vector", lambda e, bh=bh, qb=qb, po=po, gq=gq: e.scalar_tensor_tensor(
                obuf[bh][:, qb], p_o[:, po], 1.0, rb[:, gq % 2], ALU.mult, ALU.mult), waits=[(SC, sc_rb[gq])], inc=VC)
            vc_ob[po] = VC.n
            vc_stt[gq] = VC.n

        for bh in range(4):
            hl, b = bh % 2, bh // 2
            for qb in range(8):
                npair = 2 * (qb + 1)
                po = qb % 2
                if qb >= 1:
                    emit_evac(bh * 8 + qb - 1)

                def emit_av(j, gp_j, qb=qb, hl=hl, b=b, po=po):
                    last = (j == 2 * (qb + 1) - 1)
                    asl = gp_j % 4
                    for h2 in range(2):
                        i = 2 * j + h2
                        s3.add("tensor", lambda e, hl=hl, b=b, i=i, asl=asl, po=po, h2=h2, sp=(last and h2 == 1):
                               e.matmul(p_o[:, po], vbuf[b][:, hl, i], atr[:, asl, h2], start=(i == 0), stop=sp),
                               waits=[(VC, vc_p2[gp_j])] if h2 == 0 else [])
                        wden = []
                        if i == 0 and vc_dr_prev != 0:
                            pq = vc_dr_prev[1]
                            wden = [(SC, tailq[pq])]
                        s3.add("tensor", lambda e, i=i, asl=asl, h2=h2, st=(i == 0), sp=(last and h2 == 1):
                               e.matmul(p_d[:], ones_c[:], atr[:, asl, h2], start=st, stop=sp),
                               waits=wden, inc=TC if h2 == 1 else None)
                    tc_den[gp_j] = TC.n

                for j in range(npair):
                    pb0 = (2 * j) % 4
                    w = [(SC, sc_sq.get(gp - 2, 0))]
                    if j == 0:
                        w.append((VC, vc_ob[po]))
                    for h2 in range(2):
                        i = 2 * j + h2
                        s3.add("tensor", lambda e, hl=hl, b=b, i=i, qb=qb, pb=pb0 + h2: e.matmul(
                            p_s[:, pb], qkT[b][:, 2 + hl, 128 * i:128 * (i + 1)], qkT[b][:, hl, 512 * qb:512 * (qb + 1)],
                            start=True, stop=True), waits=w if h2 == 0 else [], inc=TC if h2 == 1 else None)
                    tc_st[gp] = TC.n
                    diag = 2 * j >= 4 * qb
                    ps2 = gp % 4
                    wv = [(TC, tc_st[gp]), (VC, vc_p2.get(gp - 4, 0))]
                    if pc_mask.get(gp - 4) is not None:
                        wv.append((PC, pc_mask[gp - 4]))
                    s3.add("scalar", lambda e, pb0=pb0, ps2=ps2: e.activation(
                        s2a[:, ps2].rearrange("p a n -> p (a n)"), p_s[:, pb0:pb0 + 2].rearrange("p a n -> p (a n)"),
                        AF.Square), waits=wv, inc=SC)
                    sc_sq[gp] = SC.n
                    if diag:
                        for h2 in range(2):
                            i = 2 * j + h2
                            base = qb * 512 - i * 128
                            s3.add("gpsimd", lambda e, ps2=ps2, h2=h2, base=base: e.affine_select(
                                out=s2m[:, h2], in_=s2a[:, ps2, h2], pattern=[[1, 512]],
                                compare_op=ALU.is_ge, fill=0.0, base=base, channel_multiplier=-1),
                                waits=[(SC, sc_sq[gp]), (VC, s2m_user)] if h2 == 0 else [], inc=PC if h2 == 1 else None)
                        pc_mask[gp] = PC.n
                    asl = gp % 4
                    w = [(TC, tc_den.get(gp - 4, 0))]
                    if diag:
                        w.append((PC, pc_mask[gp]))
                        sv = s2m.rearrange("p a n -> p (a n)")
                    else:
                        w.append((SC, sc_sq[gp]))
                        sv = s2a[:, ps2].rearrange("p a n -> p (a n)")
                    s3.add("vector", lambda e, sv=sv, asl=asl: e.tensor_tensor(
                        atr[:, asl].rearrange("p a n -> p (a n)"), sv, sv, ALU.mult), waits=w, inc=VC)
                    vc_p2[gp] = VC.n
                    if diag:
                        s2m_user = VC.n
                    if j >= 3:
                        emit_av(j - 3, gp - 3)
                    gp += 1
                    if qb >= 1:
                        if j == 1:
                            emit_recip(bh * 8 + qb - 1)
                        if j == npair - 2:
                            emit_tail(bh, qb - 1)
                for jf in (3, 2, 1):
                    if npair - jf >= 0:
                        emit_av(npair - jf, gp - jf)
                tc_den_last[bh * 8 + qb] = tc_den[gp - 1]
                vc_dr_prev = ("pending", bh * 8 + qb)
            emit_evac(bh * 8 + 7)
            emit_recip(bh * 8 + 7)
            emit_tail(bh, 7)
            hl2, b2 = bh % 2, bh // 2
            vc_ob_last = max(vc_ob[0], vc_ob[1])
            for jd in range(4 * b2, 4 * b2 + 4):
                nj = (jd % 4) * 1024
                s3.add("sync", lambda e, jd=jd, hl2=hl2, bh=bh, nj=nj: e.dma_start(
                    out=cc3_in[jd * 256 + hl2 * 128: jd * 256 + (hl2 + 1) * 128, :],
                    in_=obuf[bh].rearrange("p q n -> p (q n)")[:, nj:nj + 1024]),
                    waits=[(VC, vc_ob_last)] if jd == 4 * b2 else [], inc=CC3W, inck=16)
            sd_b = CC3W.n
            for jd in range(4 * b2, 4 * b2 + 4):
                s3.add("gpsimd", lambda e, jd=jd, hl2=hl2: e.collective_compute(
                    "AllGather", ALU.bypass,
                    ins=[cc3_in[jd * 256 + hl2 * 128: jd * 256 + hl2 * 128 + 128, :]],
                    outs=[cc3_out[(jd * 2 + hl2) * 1024: (jd * 2 + hl2 + 1) * 1024, :]],
                    replica_groups=RG),
                    waits=[(CC3W, sd_b)] if jd == 4 * b2 else [], inc=CCA3)
        cc_ag3 = CCA3.n

        if "d_ob0" in dbg:
            vcv = VC.n
            for i in range(4):
                s3.add("sync", lambda e, i=i: e.dma_start(out=dbg[f"d_ob{i}"], in_=obuf[i][:]),
                       waits=[(VC, vcv)], inc=DBG, inck=16)
        s3.emit(nc)
    stageB.close()
    if STAGES <= 4:
        _dev_final_barrier()
        stageC.close()
        return nc

    stageC.close()
    wout_sb = sb("wout_sb", [128, KCH, 1024], BF16)
    attnT = sb("attnT", [128, H, ROWS], BF16)
    bias_bc = sb("bias_bc", [128, E], F32)

    s45 = Sched(marks_of([x for x in ALLSEMS if x.name != "CCA3"]), allsems=ALLSEMS)
    with nc.sbuf_tensor("wsum_h", [128, KCH, 1024], BF16) as wsum_h, \
         nc.sbuf_tensor("f_full", [128, 2, 8, 2, 512], BF16) as f_full, \
         nc.sbuf_tensor("fb_row", [1, E], F32) as fb_row, \
         nc.sbuf_tensor("x_t", [128, 2, 1024], F32) as x_t, \
         nc.sbuf_tensor("xb_t", [128, 2, 1024], F32) as xb_t, \
         nc.sbuf_tensor("o_t", [128, 2, 1024], F32) as o_t, \
         nc.psum_tensor("p_a", [128, 4, 512], F32) as p_a, \
         nc.psum_tensor("p_f", [128, 4, 512], F32) as p_f:

        for hl in range(2):
            def rd(e, hl=hl):
                rank = e.cc_rank(RG)
                src = cc3_out.rearrange("(j l i d) n -> j l i d n", l=2, i=8, d=128)[ds(rank, 1), hl, :, :, :]
                src = src.rearrange("o i d n -> d (o i) n")
                dst = attnT.rearrange("p (i hl) n -> p i hl n", hl=2)[:, :, hl, :]
                return e.dma_start(out=dst, in_=src)
            s45.add("sync", rd, waits=[(CCA3, cc_ag3)] if hl == 0 else [], inc=AT, inck=16)
        sd_attnT = (AT, AT.n)
        if "d_attnT" in dbg:
            s45.add("sync", lambda e: e.dma_start(out=dbg["d_attnT"], in_=attnT[:]), waits=[sd_attnT], inc=DBG, inck=16)

        s45.add("scalar", lambda e: e.dma_start(out=fb_row[:], in_=fbias[:]), inc=ADC, inck=16)
        ad_fb = ADC.n
        sc_bb = 0
        for cchunk in range(4):
            s45.add("tensor", lambda e, cchunk=cchunk: e.matmul(
                p_f[:, 0], ones_r[:], fb_row[:, 512 * cchunk:512 * (cchunk + 1)], start=True, stop=True),
                waits=[(ADC, ad_fb), (SC, sc_bb)], inc=TC)
            tc_bb = TC.n
            s45.add("scalar", lambda e, cchunk=cchunk: e.activation(
                bias_bc[:, 512 * cchunk:512 * (cchunk + 1)], p_f[:, 0], AF.Copy), waits=[(TC, tc_bb)], inc=SC)
            sc_bb = SC.n

        ad_ws = {}
        sc_ff = {}
        tc_ff = {}
        tc_pa_d = {}
        sd_xt5 = {}
        vc_xb = {}
        vc_out = {}
        sd_out = {}
        wo_mark = {}

        def load_wsum(Hh, wtc):
            for rr in range(4):
                waw = [(CCA2, cc_ag2), (TC, wtc)]
                s45.add("scalar", lambda e, Hh=Hh, rr=rr: e.dma_start(
                    out=wsum_h.rearrange("p k (r c) -> p k r c", c=256)[:, :, rr, :],
                    in_=cc2_out[(Hh * 4 + rr) * E:(Hh * 4 + rr + 1) * E, :].rearrange("(k p) c -> p k c", p=128)),
                    waits=waw if rr == 0 else [], inc=WS, inck=16)
            ad_ws[Hh] = WS.n

        def load_wout(Hh, wtc):
            s45.add("gpsimd", lambda e, Hh=Hh: e.dma_start(
                out=wout_sb[:], in_=wout.rearrange("(k p) n -> p k n", p=128)[:, :, 1024 * Hh:1024 * (Hh + 1)]),
                waits=[(TC, wtc)], inc=WO, inck=16)
            wo_mark[Hh] = (WO, WO.n)

        def phaseA(Hh):
            for rc in range(8):
                for cb2 in range(2):
                    pslot = (rc % 2) * 2 + cb2
                    prev = sc_ff.get((Hh, rc - 2, cb2), sc_ff.get((Hh - 1, 6 + (rc % 2), cb2), 0))
                    wf = [(WS, ad_ws[Hh]), (SC, prev if prev else sc_bb)]
                    if pslot == 0:
                        wf.append((SC, sc_bb))
                    for k in range(KCH):
                        s45.add("tensor", lambda e, k=k, rc=rc, cb2=cb2, pslot=pslot: e.matmul(
                            p_f[:, pslot], xnT_own[:, k, 128 * rc:128 * (rc + 1)], wsum_h[:, k, 512 * cb2:512 * (cb2 + 1)],
                            start=(k == 0), stop=(k == KCH - 1)), waits=wf if k == 0 else [], inc=TC if k == KCH - 1 else None)
                    tc_ff[(Hh, rc, cb2)] = TC.n
                    s45.add("scalar", lambda e, Hh=Hh, rc=rc, cb2=cb2, pslot=pslot: e.activation(
                        f_full[:, Hh, rc, cb2], p_f[:, pslot], AF.Identity, scale=SQ8),
                        waits=[(TC, tc_ff[(Hh, rc, cb2)])], inc=SC)
                    sc_ff[(Hh, rc, cb2)] = SC.n

        def phaseB(Hh):
            for rc in range(8):
                xs = rc % 2
                w = [(VC, vc_xb.get((Hh, rc - 2), 0)), (VC, vc_out.get((Hh - 1, rc, 0), 0)), (VC, vc_out.get((Hh - 1, rc, 1), 0))]
                x5sem = X50 if xs == 0 else X51
                s45.add("sync", lambda e, rc=rc, Hh=Hh, xs=xs: e.dma_start(
                    out=x_t[:, xs], in_=x_rows[128 * rc:128 * (rc + 1), 1024 * Hh:1024 * (Hh + 1)]),
                    waits=w, inc=x5sem, inck=16)
                sd_xt5[(Hh, rc)] = (x5sem, x5sem.n)
                s45.add("vector", lambda e, xs=xs, Hh=Hh: e.tensor_tensor(
                    xb_t[:, xs], x_t[:, xs], bias_bc[:, 1024 * Hh:1024 * (Hh + 1)], ALU.add),
                    waits=[sd_xt5[(Hh, rc)], (SC, sc_bb)], inc=VC)
                vc_xb[(Hh, rc)] = VC.n
                for cb2 in range(2):
                    pslot = (rc % 2) * 2 + cb2
                    idx = (Hh, rc, cb2)
                    wa = [sd_attnT, (VC, vc_out.get((Hh, rc - 2, cb2), vc_out.get((Hh - 1, 6 + (rc % 2), cb2), 0))), wo_mark[Hh]]
                    wbuf = wout_sb if Hh == 0 else wsum_h
                    for k in range(KCH):
                        s45.add("tensor", lambda e, k=k, rc=rc, cb2=cb2, pslot=pslot, wbuf=wbuf: e.matmul(
                            p_a[:, pslot], attnT[:, k, 128 * rc:128 * (rc + 1)], wbuf[:, k, 512 * cb2:512 * (cb2 + 1)],
                            start=(k == 0), stop=(k == KCH - 1)), waits=wa if k == 0 else [], inc=TC if k == KCH - 1 else None)
                    tc_pa_d[idx] = TC.n
                    otg = sd_out.get((Hh, rc - 2), sd_out.get((Hh - 1, 6 + (rc % 2)), None))
                    w = [(TC, tc_pa_d[idx]), (SC, sc_ff[idx]), (VC, vc_xb[(Hh, rc)])]
                    if otg is not None:
                        w.append(otg)
                    s45.add("vector", lambda e, Hh=Hh, rc=rc, pslot=pslot, xs=xs, cb2=cb2: e.scalar_tensor_tensor(
                        o_t[:, xs, 512 * cb2:512 * (cb2 + 1)], p_a[:, pslot], 1.0, f_full[:, Hh, rc, cb2], ALU.mult, ALU.add),
                        waits=w)
                    s45.add("vector", lambda e, pslot=pslot, xs=xs, cb2=cb2: e.tensor_tensor(
                        o_t[:, xs, 512 * cb2:512 * (cb2 + 1)], o_t[:, xs, 512 * cb2:512 * (cb2 + 1)],
                        xb_t[:, xs, 512 * cb2:512 * (cb2 + 1)], ALU.add), inc=VC)
                    vc_out[idx] = VC.n
                w = [(VC, vc_out[(Hh, rc, 0)]), (VC, vc_out[(Hh, rc, 1)])]
                otsem = OT0 if xs == 0 else OT1
                s45.add("sync", lambda e, Hh=Hh, rc=rc, xs=xs: e.dma_start(
                    out=out_rows[128 * rc:128 * (rc + 1), 1024 * Hh:1024 * (Hh + 1)], in_=o_t[:, xs]),
                    waits=w, inc=otsem, inck=16)
                sd_out[(Hh, rc)] = (otsem, otsem.n)

        load_wsum(0, 0)
        load_wout(0, 0)
        phaseA(0)
        if STAGES <= 5:
            s45.emit(nc)
            _dev_final_barrier()
            return nc
        load_wsum(1, tc_ff[(0, 7, 1)])
        phaseA(1)
        s45.add("gpsimd", lambda e: e.dma_start(
            out=wsum_h[:], in_=wout.rearrange("(k p) n -> p k n", p=128)[:, :, 1024:2048]),
            waits=[(TC, tc_ff[(1, 7, 1)])], inc=WO, inck=16)
        wo_mark[1] = (WO, WO.n)
        phaseB(0)
        phaseB(1)
        s45.add("sync", lambda e: e.nop(), waits=[(OT0, OT0.n), (OT1, OT1.n), (DBG, DBG.n)])
        s45.emit(nc)

    return nc

    return nc


def _host_inputs(x, w_qkv, w_out, ln_scale, ln_bias, factors, fbias):
    x2 = np.ascontiguousarray(np.asarray(x, np.float32).reshape(NB * SEQ, E))
    w_qkv = np.asarray(w_qkv, np.float32)
    w_out = np.ascontiguousarray(np.asarray(w_out, np.float32))
    factors = np.asarray(factors, np.float32)
    pos = np.arange(SEQ, dtype=np.float64)[:, None]
    inv = 1.0 / (10000.0 ** (np.arange(0, D, 2, dtype=np.float64) / D))
    ang = pos * inv[None, :]
    rcos = np.ascontiguousarray(np.cos(ang).astype(np.float32))
    rsin = np.ascontiguousarray(np.sin(ang).astype(np.float32))
    ident_f = np.eye(128, dtype=np.float32)
    onescol = np.ones((128, 1), dtype=bfnp)
    onesrow = np.ones((1, 128), dtype=np.float32)
    in_maps = []
    for c in range(NC):
        h0, h1 = 2 * c, 2 * c + 1
        cols = []
        for base in (0, E, 2 * E):
            cols.append(w_qkv[:, base + 128 * h0: base + 128 * (h0 + 1)])
            cols.append(w_qkv[:, base + 128 * h1: base + 128 * (h1 + 1)])
        wq = np.ascontiguousarray(np.concatenate(cols, axis=1))
        in_maps.append({
            "x_rows": np.ascontiguousarray(x2[ROWS * c: ROWS * (c + 1)]),
            "wqkv": wq,
            "wout": w_out,
            "facs": np.ascontiguousarray(factors[:, :, 256 * c: 256 * (c + 1)]),
            "fbias": np.ascontiguousarray(np.asarray(fbias, np.float32).reshape(1, E)),
            "lscale": np.ascontiguousarray(np.asarray(ln_scale, np.float32).reshape(D, 1)),
            "lbias": np.ascontiguousarray(np.asarray(ln_bias, np.float32).reshape(D, 1)),
            "rcos": rcos.astype(bfnp), "rsin": rsin.astype(bfnp),
            "constv": np.tile(np.array([[EPS, SQ8, 0.0, 0.0]], dtype=np.float32), (128, 1)),
            "ident_f": ident_f, "ident_b": ident_f.astype(bfnp), "onescol": onescol, "onesrow": onesrow,
        })
    return in_maps


_RESULTS = {}


def kernel(L0_attn_x, L0_attn_w_qkv, L0_attn_w_out, L0_attn_ln_scale,
           L0_attn_ln_bias, L0_ffn_factors, L0_ffn_bias):
    in_maps = _host_inputs(L0_attn_x, L0_attn_w_qkv, L0_attn_w_out,
                           L0_attn_ln_scale, L0_attn_ln_bias,
                           L0_ffn_factors, L0_ffn_bias)
    with ExitStack() as stack:
        nc = build(stack)
        res = run_bass_kernel_spmd(nc, in_maps, core_ids=list(range(NC)))
    _RESULTS["res"] = res
    out = np.concatenate([res.results[c]["out_rows"] for c in range(NC)], axis=0)
    return out.reshape(NB, SEQ, E)
